# version 09
# baseline (speedup 1.0000x reference)
import contextlib
import math
import numpy as np
import ml_dtypes
import concourse.bass as bass
import concourse.mybir as mybir
from concourse.bass_utils import run_bass_kernel_spmd

F32 = mybir.dt.float32
BF16 = mybir.dt.bfloat16
I32 = mybir.dt.int32
AF = mybir.ActivationFunctionType
ALU = mybir.AluOpType
AX = mybir.AxisListType.X

NCORES = 8
T = 2048
TH = 1024
D = 4096
H = 32
FF = 11008
NKC = 32
NM = 86
QSCALE = 1.0 / math.sqrt(128.0)
EPS = 1e-5
ENGS = ("sync", "scalar", "vector", "tensor", "gpsimd")


class Plan:
    def __init__(self):
        self.ops = {e: [] for e in ENGS}

    def em(self, eng, fn):
        self.ops[eng].append(fn)


class Sem:
    __slots__ = ("h", "v", "name")

    def __init__(self, h, name):
        self.h = h
        self.v = 0
        self.name = name


def build_nc(debug=False, skip_tail=False, skip_p5=False):
    nc = bass.Bass("TRN2", target_bir_lowering=False)
    P = Plan()

    x_in = nc.dram_tensor("x", [T, D], F32, kind="ExternalInput").ap()
    w_in = {w: nc.dram_tensor(f"w{w}", [D, D], F32, kind="ExternalInput").ap() for w in "qkv"}
    b_in = {w: nc.dram_tensor(f"b{w}", [D], F32, kind="ExternalInput").ap() for w in "qkv"}
    qwg = nc.dram_tensor("qwg", [512, FF], I32, kind="ExternalInput").ap()
    scg = nc.dram_tensor("scg", [32, FF], F32, kind="ExternalInput").ap()
    qzg = nc.dram_tensor("qzg", [32, FF // 8], I32, kind="ExternalInput").ap()
    qwu = nc.dram_tensor("qwu", [512, FF], I32, kind="ExternalInput").ap()
    scu = nc.dram_tensor("scu", [32, FF], F32, kind="ExternalInput").ap()
    qzu = nc.dram_tensor("qzu", [32, FF // 8], I32, kind="ExternalInput").ap()
    qwd = nc.dram_tensor("qwd", [FF // 8, D], I32, kind="ExternalInput").ap()
    scd = nc.dram_tensor("scd", [NM, D], F32, kind="ExternalInput").ap()
    qzd = nc.dram_tensor("qzd", [NM, D // 8], I32, kind="ExternalInput").ap()
    identb_in = nc.dram_tensor("identb", [128, 128], BF16, kind="ExternalInput").ap()
    identf_in = nc.dram_tensor("identf", [128, 128], F32, kind="ExternalInput").ap()
    maskl_in = nc.dram_tensor("maskl", [5, 128], BF16, kind="ExternalInput").ap()
    maskr_in = nc.dram_tensor("maskr", [5, 128], BF16, kind="ExternalInput").ap()
    shift8_in = nc.dram_tensor("shift8", [128, 1], I32, kind="ExternalInput").ap()
    out_t = nc.dram_tensor("out", [T, D], F32, kind="ExternalOutput").ap()

    sk = "ExternalOutput" if debug else "Internal"
    wbf = {w: nc.dram_tensor(f"wbf_{w}", [D, D], BF16, kind="Internal").ap() for w in "qkv"}
    wT = {w: nc.dram_tensor(f"wT_{w}", [D, D], BF16, kind=sk if w == "q" else "Internal").ap() for w in "qkv"}
    Wg_dq = nc.dram_tensor("Wg_dq", [D, FF], BF16, kind=sk).ap()
    Wu_dq = nc.dram_tensor("Wu_dq", [D, FF], BF16, kind="Internal").ap()
    Wd_dq = nc.dram_tensor("Wd_dq", [FF, D], BF16, kind=sk).ap()
    QT = nc.dram_tensor("QT", [D, T], BF16, kind=sk).ap()
    KT = nc.dram_tensor("KT", [D, T], BF16, kind=sk).ap()
    VT = nc.dram_tensor("VT", [D, T], BF16, kind=sk).ap()
    HT = nc.dram_tensor("HT", [FF, T], BF16, kind=sk).ap()
    GT = nc.dram_tensor("GT", [D, T], BF16, kind=sk).ap()
    sT_h = nc.dram_tensor("sT_h", [128, 16 * D], BF16, kind=sk).ap()
    z_n = {"g": nc.dram_tensor("zg_n", [32, FF], F32, kind="Internal").ap(),
           "u": nc.dram_tensor("zu_n", [32, FF], F32, kind="Internal").ap(),
           "d": nc.dram_tensor("zd_n", [NM, D], F32, kind="Internal").ap()}
    xnT_dbg = nc.dram_tensor("xnT_dbg", [D, T], BF16, kind=sk).ap() if debug else None
    sT_dbg = nc.dram_tensor("sT_dbg", [128, 65536], BF16, kind=sk).ap() if debug else None

    with contextlib.ExitStack() as S:
        R1 = S.enter_context(nc.sbuf_tensor("R1", [128, 32768], BF16))
        R2 = S.enter_context(nc.sbuf_tensor("R2", [128, 16384], F32))
        SB = S.enter_context(nc.sbuf_tensor("SB", [128, 16384], BF16))
        ps = S.enter_context(nc.psum_tensor("ps", [128, 4096], F32))

        def xnT(c):
            return R1[:, 1024 * c:1024 * (c + 1)]

        QTs = R1[:, 0:8192]
        KTs = R1[:, 8192:16384]
        VTs = R1[:, 16384:24576]
        sT = R1[:, 24576:32768]
        sgT = QTs

        xbuf = [R2[:, 0:4096], R2[:, 4096:8192]]
        xsq = R2[:, 8192:12288]
        xnb = [R2[:, 12288:14336].bitcast(BF16), R2[:, 14336:16384].bitcast(BF16)]
        z4t = [R2[:, 0:344].bitcast(I32), R2[:, 344:688].bitcast(I32)]
        znat = R2[:, 4096:6848]
        dq_qw = R2[:, 0:2752].bitcast(I32)
        dq_w4 = R2[:, 2752:5504].bitcast(I32)
        dq_z = R2[:, 5504:6880].bitcast(BF16)
        dq_s = R2[:, 6880:8256].bitcast(BF16)
        dq_T = [R2[:, 8256:9632].bitcast(BF16), R2[:, 9632:11008].bitcast(BF16)]
        dd_qw = R2[:, 0:4096].bitcast(I32)
        dd_w4 = R2[:, 4096:8192].bitcast(I32)
        dd_z = R2[:, 8192:10240].bitcast(BF16)
        dd_s = R2[:, 10240:12288].bitcast(BF16)
        dd_T = [R2[:, 12288:14336].bitcast(BF16), R2[:, 14336:16384].bitcast(BF16)]
        outb = [R2[:, 8192:12288], R2[:, 12288:16384]]

        wtr = [SB[:, 12288:16384]]
        wstat = [SB[:, 0:4096], SB[:, 4096:8192], SB[:, 8192:12288]]
        gstat = [SB[:, 0:4096], SB[:, 4096:8192]]
        ustat = [SB[:, 8192:12288], SB[:, 12288:16384]]
        wdbat = [SB[:, 0:4096], SB[:, 4096:8192]]
        htb = [SB[:, 8192:10240], SB[:, 10240:12288], SB[:, 12288:14336]]
        gtstg = [SB[:, 14336:15360], SB[:, 15360:16384]]

        qstg = [S.enter_context(nc.sbuf_tensor(f"qstg{i}", [128, 1024], BF16)) for i in range(2)]
        siluG = [S.enter_context(nc.sbuf_tensor(f"siluG{i}", [128, 1024], BF16)) for i in range(2)]
        hstg = [S.enter_context(nc.sbuf_tensor(f"hstg{i}", [128, 1024], BF16)) for i in range(2)]
        id_b = S.enter_context(nc.sbuf_tensor("id_b", [128, 128], BF16))
        id_f = S.enter_context(nc.sbuf_tensor("id_f", [128, 128], F32))
        mskl = S.enter_context(nc.sbuf_tensor("mskl", [5, 128], BF16))
        mskr = S.enter_context(nc.sbuf_tensor("mskr", [5, 128], BF16))
        sh8 = S.enter_context(nc.sbuf_tensor("sh8", [128, 1], I32))
        braw = S.enter_context(nc.sbuf_tensor("braw", [32, 128], F32))
        bt = {w: S.enter_context(nc.sbuf_tensor(f"bt_{w}", [128, 32], F32)) for w in "qkv"}
        qzs = S.enter_context(nc.sbuf_tensor("qzs", [NM, FF // 8], I32))
        stt2 = [{n: S.enter_context(nc.sbuf_tensor(f"st_{n}{i}", [128, 1], F32))
                 for n in ("s1", "s2", "mu", "m2", "mu2", "ve", "ln", "rs", "nm")}
                for i in range(2)]
        qkw = S.enter_context(nc.sbuf_tensor("qkw", [128, 1024], BF16))
        kkw = S.enter_context(nc.sbuf_tensor("kkw", [128, 1024], BF16))
        vpw = S.enter_context(nc.sbuf_tensor("vpw", [128, 1024], BF16))
        vkw = S.enter_context(nc.sbuf_tensor("vkw", [128, 1024], BF16))
        Ew = S.enter_context(nc.sbuf_tensor("Ew", [128, 1024], F32))
        Pw = S.enter_context(nc.sbuf_tensor("Pw", [128, 1024], BF16))
        PTw = S.enter_context(nc.sbuf_tensor("PTw", [128, 1024], BF16))
        sumw = S.enter_context(nc.sbuf_tensor("sumw", [128, 8], F32))
        rcpw = S.enter_context(nc.sbuf_tensor("rcpw", [128, 8], F32))
        qstk = [S.enter_context(nc.sbuf_tensor(f"qstk{i}", [128, 128], BF16)) for i in range(2)]
        kstk = [S.enter_context(nc.sbuf_tensor(f"kstk{i}", [128, 128], BF16)) for i in range(2)]
        vpre = [S.enter_context(nc.sbuf_tensor(f"vpre{i}", [128, 128], BF16)) for i in range(2)]
        vstk = [S.enter_context(nc.sbuf_tensor(f"vstk{i}", [128, 128], BF16)) for i in range(2)]
        Esb = [S.enter_context(nc.sbuf_tensor(f"Esb{i}", [128, 128], F32)) for i in range(2)]
        Pbd = [S.enter_context(nc.sbuf_tensor(f"Pbd{i}", [128, 128], BF16)) for i in range(2)]
        PTbd = [S.enter_context(nc.sbuf_tensor(f"PTbd{i}", [128, 128], BF16)) for i in range(2)]
        nmax = [S.enter_context(nc.sbuf_tensor(f"nmax{i}", [128, 1], F32)) for i in range(2)]
        ssum = [S.enter_context(nc.sbuf_tensor(f"ssum{i}", [128, 1], F32)) for i in range(2)]
        rcp = [S.enter_context(nc.sbuf_tensor(f"rcp{i}", [128, 1], F32)) for i in range(2)]

        trp = [ps[:, 3072:3136].bitcast(BF16), ps[:, 3584:3648].bitcast(BF16)]
        trpf = ps[:, 3072:3104]
        psQ4 = [ps[:, 0:1024], ps[:, 1024:2048], ps[:, 2048:3072], ps[:, 3072:4096]]
        psG = [ps[:, 0:1024], ps[:, 2048:3072]]
        psU = [ps[:, 1024:2048], ps[:, 3072:4096]]
        psD = [ps[:, 0:2048], ps[:, 2048:4096]]
        ps_s = [ps[:, 0:128], ps[:, 1536:1664]]
        vps = [ps[:, 512:576].bitcast(BF16), ps[:, 2048:2112].bitcast(BF16)]
        ps_c = [ps[:, 1024:1152], ps[:, 2560:2688]]

        names = ["d_small", "d_wcast", "d_wtr0", "d_wtr1", "d_wsc0", "d_wsc1",
                 "d_x0", "d_x1", "d_stat0", "d_stat1", "d_stat2", "d_qw0", "d_qw1",
                 "d_rep0", "d_rep1", "d_dqo0", "d_dqo1", "d_qkv0", "d_qkv1",
                 "d_sg0", "d_sg1", "d_su0", "d_su1", "d_ht_w", "d_hr0", "d_hr1", "d_hr2",
                 "d_wd0", "d_wd1", "d_gt0", "d_gt1", "d_bank", "d_out", "d_zw", "d_stw",
                 "v_misc", "v_ln", "v_ev", "v_dq", "v_hm", "v_dep", "v_p3", "v_m5", "v_a5",
                 "a_misc", "a_ln", "a_xn", "a_ev1", "a_silu", "a_silu4", "a_exp", "a_pm",
                 "a_vk", "a_e5", "t_misc", "t_xp", "t_mm1", "t_mm2", "t_mm4", "t_sc",
                 "t_vt", "t_cx", "t_t5", "g_stk", "a_vst"]
        Z = {n: Sem(S.enter_context(nc.semaphore(n)), n) for n in names}

        def dma(q, out, in_, sem, **kw):
            sem.v += 16
            P.em(q, lambda e, out=out, in_=in_, sem=sem, kw=kw:
                 e.dma_start(out=out, in_=in_, **kw).then_inc(sem.h, 16))
            return sem.v

        def wait(eng, sem, v):
            if v > 0:
                P.em(eng, lambda e, sem=sem, v=v: e.wait_ge(sem.h, v))

        def inc(eng, fn, sem, n=1):
            sem.v += n
            P.em(eng, lambda e, fn=fn, sem=sem, n=n: fn(e).then_inc(sem.h, n))
            return sem.v

        def op(eng, fn):
            P.em(eng, lambda e, fn=fn: fn(e))

        dma("sync", id_b[:, :], identb_in[:, :], Z["d_small"])
        dma("sync", id_f[:, :], identf_in[:, :], Z["d_small"])
        dma("sync", mskl[:, :], maskl_in[:, :], Z["d_small"])
        dma("sync", mskr[:, :], maskr_in[:, :], Z["d_small"])
        dma("sync", sh8[:, :], shift8_in[:, :], Z["d_small"])
        tv_prev = 0
        for w, scale in (("q", QSCALE), ("k", 1.0), ("v", 1.0)):
            wait("sync", Z["t_misc"], tv_prev)
            lv = dma("sync", braw[:, :], b_in[w].rearrange("(g p) -> g p", p=128), Z["d_small"])
            wait("tensor", Z["d_small"], lv)
            tv_prev = inc("tensor", lambda e: e.transpose(trpf[:, :], braw[:, :], id_f[0:32, 0:32]), Z["t_misc"])
            wait("scalar", Z["t_misc"], tv_prev)
            av = inc("scalar", lambda e, w=w, scale=scale: e.activation(
                out=bt[w][:, :], in_=trpf[:, :], func=AF.Identity, bias=0.0, scale=float(scale)),
                Z["a_misc"])
            wait("tensor", Z["a_misc"], av)

        for key, qz_src, G, NN in (("g", qzg, 32, FF), ("u", qzu, 32, FF), ("d", qzd, NM, D)):
            wait("sync", Z["v_misc"], Z["v_misc"].v)
            lv = dma("sync", qzs[0:G, 0:NN // 8], qz_src[:, :], Z["d_small"])
            wait("vector", Z["d_small"], lv)
            NS = 344 if NN == FF else 128
            nsl = (NN // 8) // NS
            for s in range(nsl):
                wait("vector", Z["d_zw"], Z["d_zw"].v)
                for rb in range(8):
                    pb = rb % 2
                    wait("vector", Z["v_misc"], max(0, Z["v_misc"].v - 2))
                    uv = inc("vector", lambda e, s=s, rb=rb, pb=pb, G=G, NS=NS: e.tensor_scalar(
                        out=z4t[pb][0:G, 0:NS], in0=qzs[0:G, NS * s:NS * (s + 1)],
                        scalar1=4 * rb, scalar2=15, op0=ALU.logical_shift_right, op1=ALU.bitwise_and),
                        Z["v_misc"])
                    wait("vector", Z["v_misc"], uv)
                    inc("vector", lambda e, s=s, rb=rb, pb=pb, G=G, NS=NS: e.tensor_scalar(
                        out=znat[0:G, 0:8 * NS].rearrange("g (n e) -> g n e", e=8)[:, :, rb:rb + 1],
                        in0=z4t[pb][0:G, 0:NS].rearrange("g (n o) -> g n o", o=1),
                        scalar1=1, scalar2=None, op0=ALU.add), Z["v_misc"])
                wait("gpsimd", Z["v_misc"], Z["v_misc"].v)
                dma("gpsimd", z_n[key][0:G, 8 * NS * s:8 * NS * (s + 1)], znat[0:G, 0:8 * NS], Z["d_zw"])
        zw_all = Z["d_zw"].v
        vmisc_all = Z["v_misc"].v

        for cc in range(4):
            dma("gpsimd", wbf["q"][:, 1024 * cc:1024 * (cc + 1)],
                w_in["q"][:, 1024 * cc:1024 * (cc + 1)], Z["d_wcast"])
        for wi, w in enumerate("kv"):
            for cc in range(4):
                wait("gpsimd", Z["t_mm1"], max(0, 32 * wi + 8 * (cc - 1)))
                dma("gpsimd", wbf[w][:, 1024 * cc:1024 * (cc + 1)],
                    w_in[w][:, 1024 * cc:1024 * (cc + 1)], Z["d_wcast"])
        d_tr = [Z["d_wtr0"], Z["d_wtr1"]]
        d_sc = [Z["d_wsc0"], Z["d_wsc1"]]
        _wpair_state = {"w": None, "trv": [0, 0]}

        _trv = {}
        NWTR = len(wtr)

        def emit_wtr(w, a):
            base = {"q": 0, "k": 4, "v": 8}[w]
            wait("sync", Z["d_wcast"], 16 * (base + a // 8 + 1))
            pb = a % NWTR
            wait("sync", d_sc[pb], d_sc[pb].v)
            dma("sync", wtr[pb][:, :], wbf[w][:, 128 * a:128 * (a + 1)], d_tr[pb], transpose=True)
            _trv[(w, a)] = d_tr[pb].v

        def emit_wsc(w, a):
            pb = a % NWTR
            wait("sync", d_tr[pb], _trv[(w, a)])
            base_s = 128 * (a // 8) + 16 * (a % 8)
            dst = wT[w].rearrange("(kb s) h -> s kb h", kb=8)[base_s:base_s + 16, :, :]
            dma("sync", dst, wtr[pb][:, :], d_sc[pb])

        def emit_wpair(w, a, q=None):
            emit_wtr(w, a)
            emit_wsc(w, a)

        def emit_wpairs_drain():
            wait("sync", d_sc[0], d_sc[0].v)
            wait("sync", d_sc[1], d_sc[1].v)

        d_x = [Z["d_x0"], Z["d_x1"]]

        def emit_p0c_half(h):
            if h == 1:
                wait("scalar", Z["v_dq"], Z["v_dq"].v)
                wait("scalar", Z["d_dqo0"], Z["d_dqo0"].v)
                wait("scalar", Z["d_dqo1"], Z["d_dqo1"].v)
                wait("tensor", Z["v_hm"], Z["v_hm"].v)
                wait("vector", Z["t_mm2"], Z["t_mm2"].v)
            else:
                wait("scalar", Z["v_misc"], vmisc_all)
                wait("scalar", Z["d_zw"], zw_all)
            xv = [0, 0]
            for i in range(8):
                ti = 8 * h + i
                pb = i % 2
                if i < 2:
                    xv[pb] = dma("scalar", xbuf[pb][:, :], x_in[128 * ti:128 * (ti + 1), :], d_x[pb])
                wait("scalar", d_x[pb], xv[pb])
                stt = stt2[pb]
                sqv = inc("scalar", lambda e, pb=pb: e.activation(
                    out=xsq[:, :], in_=xbuf[pb][:, :], func=AF.Square), Z["a_ln"])
                wait("vector", Z["a_ln"], sqv)
                wait("vector", Z["a_xn"], max(0, Z["a_xn"].v - 1))
                op("vector", lambda e: e.reduce_sum(out=stt["s2"][:, :], in_=xsq[:, :], axis=AX))
                r1v = inc("vector", lambda e, pb=pb: e.reduce_sum(
                    out=stt["s1"][:, :], in_=xbuf[pb][:, :], axis=AX), Z["v_ln"])
                wait("vector", Z["v_ln"], r1v)
                op("vector", lambda e: e.tensor_scalar(
                    out=stt["mu"][:, :], in0=stt["s1"][:, :], scalar1=1.0 / D, scalar2=None, op0=ALU.mult))
                m2v = inc("vector", lambda e: e.tensor_scalar(
                    out=stt["m2"][:, :], in0=stt["s2"][:, :], scalar1=1.0 / D, scalar2=None, op0=ALU.mult),
                    Z["v_ln"])
                wait("vector", Z["v_ln"], m2v)
                mu2v = inc("vector", lambda e: e.tensor_tensor(
                    out=stt["mu2"][:, :], in0=stt["mu"][:, :], in1=stt["mu"][:, :], op=ALU.mult), Z["v_ln"])
                wait("vector", Z["v_ln"], mu2v)
                vev = inc("vector", lambda e: e.scalar_tensor_tensor(
                    out=stt["ve"][:, :], in0=stt["m2"][:, :], scalar=EPS, in1=stt["mu2"][:, :],
                    op0=ALU.add, op1=ALU.subtract), Z["v_ln"])
                wait("scalar", Z["v_ln"], vev)
                lnv = inc("scalar", lambda e: e.activation(
                    out=stt["ln"][:, :], in_=stt["ve"][:, :], func=AF.Ln), Z["a_ln"])
                wait("scalar", Z["a_ln"], lnv)
                rsv = inc("scalar", lambda e: e.activation(
                    out=stt["rs"][:, :], in_=stt["ln"][:, :], func=AF.Exp, bias=0.0, scale=-0.5), Z["a_ln"])
                wait("vector", Z["a_ln"], rsv)
                nmv = inc("vector", lambda e: e.scalar_tensor_tensor(
                    out=stt["nm"][:, :], in0=stt["mu"][:, :], scalar=-1.0, in1=stt["rs"][:, :],
                    op0=ALU.mult, op1=ALU.mult), Z["v_ln"])
                wait("scalar", Z["v_ln"], nmv)
                wait("scalar", Z["t_xp"], max(0, Z["t_xp"].v - NKC))
                xnv = inc("scalar", lambda e, pb=pb: e.activation(
                    out=xnb[pb][:, :], in_=xbuf[pb][:, :], func=AF.Identity,
                    bias=stt["nm"][:, 0:1], scale=stt["rs"][:, 0:1]), Z["a_xn"])
                if i + 2 < 8:
                    wait("scalar", Z["a_xn"], xnv)
                    xv[pb] = dma("scalar", xbuf[pb][:, :], x_in[128 * (ti + 2):128 * (ti + 3), :], d_x[pb])
                wait("tensor", Z["a_xn"], xnv)
                for c in range(NKC):
                    kbit, kblk = c // 4, c % 4
                    tb = c % 2
                    wait("tensor", Z["v_ev"], max(0, Z["v_ev"].v - 1))
                    src = xnb[pb][:, :].rearrange("p (a r e) -> p e a r", a=4, e=8)[:, kbit, kblk, :]
                    tpv = inc("tensor", lambda e, src=src, tb=tb: e.transpose(
                        trp[tb][:, :], src, id_b[:, :]), Z["t_xp"])
                    wait("vector", Z["t_xp"], tpv)
                    inc("vector", lambda e, c=c, i=i, tb=tb: e.tensor_copy(
                        out=xnT(c)[:, 128 * i:128 * (i + 1)], in_=trp[tb][:, :]), Z["v_ev"])
            if debug:
                wait("gpsimd", Z["v_ev"], Z["v_ev"].v)
                for c in range(NKC):
                    dma("gpsimd", xnT_dbg[128 * c:128 * (c + 1), TH * h:TH * (h + 1)],
                        xnT(c)[:, :], Z["d_zw"])

        def emit_dequant_gateup():
            d_qw = [Z["d_qw0"], Z["d_qw1"]]
            d_rep = [Z["d_rep0"], Z["d_rep1"]]
            d_dqo = [Z["d_dqo0"], Z["d_dqo1"]]
            SLW = 2752
            wait("gpsimd", Z["d_zw"], zw_all)
            wait("gpsimd", Z["a_xn"], Z["a_xn"].v)
            sl_done = []
            first = True
            for sl in range(4):
                n0 = SLW * sl
                for mat, (qw_s, sc_s, zkey, W_dst) in enumerate(
                        ((qwg, scg, "g", Wg_dq), (qwu, scu, "u", Wu_dq))):
                    for kblk in range(4):
                        lb = (mat * 4 + kblk) % 2
                        if not first:
                            wait("gpsimd", Z["v_dq"], Z["v_dq"].v)
                        first = False
                        qv = dma("gpsimd", dq_qw[:, :],
                                 qw_s[128 * kblk:128 * (kblk + 1), n0:n0 + SLW], d_qw[lb])
                        dma("gpsimd", dq_z[:, :],
                            z_n[zkey][8 * kblk:8 * kblk + 8, n0:n0 + SLW]
                            .unsqueeze(1).broadcast_to([8, 16, SLW]), d_rep[lb])
                        rv = dma("gpsimd", dq_s[:, :],
                                 sc_s[8 * kblk:8 * kblk + 8, n0:n0 + SLW]
                                 .unsqueeze(1).broadcast_to([8, 16, SLW]), d_rep[lb])
                        wait("vector", d_qw[lb], qv)
                        wait("vector", d_rep[lb], rv)
                        for kbit in range(8):
                            tp = kbit % 2
                            uv = inc("vector", lambda e, kbit=kbit: e.tensor_scalar(
                                out=dq_w4[:, :], in0=dq_qw[:, :], scalar1=4 * kbit, scalar2=15,
                                op0=ALU.logical_shift_right, op1=ALU.bitwise_and), Z["v_dq"])
                            wait("vector", Z["v_dq"], uv)
                            wait("vector", d_dqo[tp], d_dqo[tp].v)
                            sv = inc("vector", lambda e, tp=tp: e.tensor_tensor(
                                out=dq_T[tp][:, :], in0=dq_w4[:, :], in1=dq_z[:, :],
                                op=ALU.subtract), Z["v_dq"])
                            wait("vector", Z["v_dq"], sv)
                            mv = inc("vector", lambda e, tp=tp: e.tensor_tensor(
                                out=dq_T[tp][:, :], in0=dq_T[tp][:, :], in1=dq_s[:, :],
                                op=ALU.mult), Z["v_dq"])
                            wait("gpsimd", Z["v_dq"], mv)
                            c = kbit * 4 + kblk
                            dma("gpsimd", W_dst[128 * c:128 * (c + 1), n0:n0 + SLW],
                                dq_T[tp][:, :], d_dqo[tp])
                sl_done.append((Z["d_dqo0"].v, Z["d_dqo1"].v))
            return sl_done

        def emit_dequant_down_chunk(m):
            d_qw = [Z["d_qw0"], Z["d_qw1"]]
            d_rep = [Z["d_rep0"], Z["d_rep1"]]
            d_dqo = [Z["d_dqo0"], Z["d_dqo1"]]
            lb = m % 2
            tp = m % 2
            wait("gpsimd", Z["a_xn"], Z["a_xn"].v)
            wait("gpsimd", Z["v_dq"], Z["v_dq"].v)
            qv = dma("gpsimd", dd_qw[:, :],
                     qwd[16 * m:16 * (m + 1), :].unsqueeze(1).broadcast_to([16, 8, D]), d_qw[lb])
            dma("gpsimd", dd_z[:, :], z_n["d"][m:m + 1, :].partition_broadcast(128), d_rep[lb])
            rv = dma("gpsimd", dd_s[:, :], scd[m:m + 1, :].partition_broadcast(128), d_rep[lb])
            wait("vector", d_qw[lb], qv)
            wait("vector", d_rep[lb], rv)
            uv = inc("vector", lambda e: e.tensor_scalar(
                out=dd_w4[:, :], in0=dd_qw[:, :], scalar1=sh8[:, 0:1], scalar2=15,
                op0=ALU.logical_shift_right, op1=ALU.bitwise_and), Z["v_dq"])
            wait("vector", Z["v_dq"], uv)
            wait("vector", d_dqo[tp], d_dqo[tp].v)
            sv = inc("vector", lambda e, tp=tp: e.tensor_tensor(
                out=dd_T[tp][:, :], in0=dd_w4[:, :], in1=dd_z[:, :], op=ALU.subtract), Z["v_dq"])
            wait("vector", Z["v_dq"], sv)
            mv = inc("vector", lambda e, tp=tp: e.tensor_tensor(
                out=dd_T[tp][:, :], in0=dd_T[tp][:, :], in1=dd_s[:, :], op=ALU.mult), Z["v_dq"])
            wait("gpsimd", Z["v_dq"], mv)
            dma("gpsimd", Wd_dq[128 * m:128 * (m + 1), :], dd_T[tp][:, :], d_dqo[tp])

        def emit_p1_matrix(h, w, next_pairs=None):
            d_stat = [Z["d_stat0"], Z["d_stat1"], Z["d_stat2"]]
            d_qkv = [Z["d_qkv0"], Z["d_qkv1"]]
            dst_of = {"q": QT, "k": KT, "v": VT}
            wait("sync", d_sc[0], d_sc[0].v)
            wait("sync", d_sc[1], d_sc[1].v)
            for m in range(32):
                grp = Z["t_mm1"].v
                pb = grp % 3
                if grp >= 3:
                    wait("sync", Z["t_mm1"], grp - 2)
                sv = dma("sync", wstat[pb][:, :].rearrange("p (c j) -> p c j", c=32),
                         wT[w].rearrange("(c p) hd -> p c hd", p=128)[:, :, 128 * m:128 * (m + 1)],
                         d_stat[pb])
                if next_pairs is not None:
                    emit_wtr(next_pairs, m)
                    emit_wsc(next_pairs, m)
                wait("tensor", d_stat[pb], sv)
                par = grp % 4
                if grp >= 4:
                    wait("tensor", Z["a_ev1"], Z["a_ev1"].v - 3)
                P.em("tensor", lambda e, pb=pb, par=par: [
                    e.matmul(psQ4[par][:, 512 * b:512 * (b + 1)],
                             wstat[pb][:, 128 * c:128 * (c + 1)],
                             xnT(c)[:, 512 * b:512 * (b + 1)],
                             start=(c == 0), stop=(c == NKC - 1))
                    for c in range(NKC) for b in range(2)][-1].then_inc(Z["t_mm1"].h, 1))
                Z["t_mm1"].v += 1
                wait("scalar", Z["t_mm1"], Z["t_mm1"].v)
                sp = grp % 2
                wait("scalar", d_qkv[sp], d_qkv[sp].v)
                ev = inc("scalar", lambda e, par=par, sp=sp, w=w, m=m: e.activation(
                    out=qstg[sp][:, :], in_=psQ4[par][:, :], func=AF.Identity,
                    bias=bt[w][:, m:m + 1], scale=(QSCALE if w == "q" else 1.0)), Z["a_ev1"])
                wait("scalar", Z["a_ev1"], ev)
                dma("scalar", dst_of[w][128 * m:128 * (m + 1), TH * h:TH * (h + 1)],
                    qstg[sp][:, :], d_qkv[sp])

        def emit_p1_half(h):
            wait("tensor", Z["v_ev"], Z["v_ev"].v)
            if h == 1:
                wait("tensor", Z["a_silu"], Z["a_silu"].v)
                wait("tensor", Z["v_hm"], Z["v_hm"].v)
                wait("sync", Z["t_mm2"], Z["t_mm2"].v)
                for w in "qkv":
                    emit_p1_matrix(h, w)
            else:
                for a in range(32):
                    emit_wpair("q", a)
                emit_p1_matrix(h, "q", next_pairs="k")
                emit_p1_matrix(h, "k", next_pairs="v")
                emit_p1_matrix(h, "v")

        def emit_p2_half(h, sl_done):
            d_sg = [Z["d_sg0"], Z["d_sg1"]]
            d_su = [Z["d_su0"], Z["d_su1"]]
            sl_waited = [False] * 4
            wait("tensor", Z["a_ev1"], Z["a_ev1"].v)
            wait("sync", Z["t_mm1"], Z["t_mm1"].v)
            for m in range(NM):
                mg = Z["v_hm"].v
                par = mg % 2
                sl = min(3, (128 * (m + 1) - 1) // 2752)
                if not sl_waited[sl]:
                    wait("sync", Z["d_dqo0"], sl_done[sl][0])
                    wait("sync", Z["d_dqo1"], sl_done[sl][1])
                    sl_waited[sl] = True
                if mg >= 2:
                    wait("sync", Z["t_mm2"], Z["t_mm2"].v - 2)
                gv = dma("sync", gstat[par][:, :].rearrange("p (c j) -> p c j", c=32),
                         Wg_dq.rearrange("(c p) n -> p c n", p=128)[:, :, 128 * m:128 * (m + 1)],
                         d_sg[par])
                uv = dma("sync", ustat[par][:, :].rearrange("p (c j) -> p c j", c=32),
                         Wu_dq.rearrange("(c p) n -> p c n", p=128)[:, :, 128 * m:128 * (m + 1)],
                         d_su[par])
                wait("tensor", d_sg[par], gv)
                if mg >= 2:
                    wait("tensor", Z["a_silu"], Z["a_silu"].v - 1)
                P.em("tensor", lambda e, par=par: [
                    e.matmul(psG[par][:, 512 * b:512 * (b + 1)],
                             gstat[par][:, 128 * c:128 * (c + 1)],
                             xnT(c)[:, 512 * b:512 * (b + 1)],
                             start=(c == 0), stop=(c == NKC - 1))
                    for c in range(NKC) for b in range(2)][-1].then_inc(Z["t_mm2"].h, 1))
                Z["t_mm2"].v += 1
                gdone = Z["t_mm2"].v
                wait("tensor", d_su[par], uv)
                if mg >= 2:
                    wait("tensor", Z["v_hm"], Z["v_hm"].v - 1)
                P.em("tensor", lambda e, par=par: [
                    e.matmul(psU[par][:, 512 * b:512 * (b + 1)],
                             ustat[par][:, 128 * c:128 * (c + 1)],
                             xnT(c)[:, 512 * b:512 * (b + 1)],
                             start=(c == 0), stop=(c == NKC - 1))
                    for c in range(NKC) for b in range(2)][-1].then_inc(Z["t_mm2"].h, 1))
                Z["t_mm2"].v += 1
                udone = Z["t_mm2"].v
                wait("scalar", Z["t_mm2"], gdone)
                if mg >= 2:
                    wait("scalar", Z["v_hm"], Z["v_hm"].v - 1)
                inc("scalar", lambda e, par=par: e.activation(
                    out=siluG[par][:, :], in_=psG[par][:, :], func=AF.Silu), Z["a_silu"])
                wait("vector", Z["t_mm2"], udone)
                wait("vector", Z["a_silu"], Z["a_silu"].v)
                sp = mg % 2
                wait("vector", Z["d_ht_w"], Z["d_ht_w"].v)
                hv = inc("vector", lambda e, par=par, sp=sp: e.tensor_tensor(
                    out=hstg[sp][:, :], in0=psU[par][:, :], in1=siluG[par][:, :], op=ALU.mult),
                    Z["v_hm"])
                wait("scalar", Z["v_hm"], hv)
                dma("scalar", HT[128 * m:128 * (m + 1), TH * h:TH * (h + 1)], hstg[sp][:, :],
                    Z["d_ht_w"])
                if m % 2 == 1:
                    dm = (43 * h) + (m - 1) // 2
                    if dm < NM:
                        emit_dequant_down_chunk(dm)

        psD4 = [ps[:, 1024 * c:1024 * (c + 1)] for c in range(4)]
        scw = ps[:, 2048:3072]
        vpsw = ps[:, 3584:4096].bitcast(BF16)

        def QTb(s):
            return R1[:, 16384 * s + 0:16384 * s + 4096]

        def KTb(s):
            return R1[:, 16384 * s + 4096:16384 * s + 8192]

        def VTb(s):
            return R1[:, 16384 * s + 8192:16384 * s + 12288]

        def sTb(s):
            return R1[:, 16384 * s + 12288:16384 * s + 16384]

        def emit_bank_load(sb, first=False):
            s = sb % 2
            if not first:
                wait("sync", Z["g_stk"], Z["g_stk"].v)
                wait("sync", Z["a_vst"], Z["a_vst"].v)
            lv = 0
            for (dstf, srcd) in ((QTb, QT), (KTb, KT), (VTb, VT)):
                lv = dma("sync", dstf(s).rearrange("p (c t) -> p c t", c=32),
                         srcd.rearrange("(c p) t -> p c t", p=128)[:, :, 128 * sb:128 * (sb + 1)],
                         Z["d_bank"])
            return lv

        def emit_p4():
            d_hr = [Z["d_hr0"], Z["d_hr1"], Z["d_hr2"]]
            d_wd = [Z["d_wd0"], Z["d_wd1"]]
            d_gt = [Z["d_gt0"], Z["d_gt1"]]
            wait("gpsimd", Z["d_ht_w"], Z["d_ht_w"].v)
            wait("sync", Z["d_dqo0"], Z["d_dqo0"].v)
            wait("sync", Z["d_dqo1"], Z["d_dqo1"].v)
            wait("sync", Z["t_mm2"], Z["t_mm2"].v)
            wait("tensor", Z["a_silu"], Z["a_silu"].v)
            wait("tensor", Z["v_hm"], Z["v_hm"].v)
            wait("tensor", Z["a_ev1"], Z["a_ev1"].v)
            for th in range(2):
                for hq in range(8):
                    g2 = th * 8 + hq
                    hp0 = 512 * hq
                    bat_v = {}

                    def load_bat(gg, hp0=hp0, g2=g2):
                        pb = (g2 * 11 + gg) % 2
                        wait("sync", Z["t_mm4"], Z["t_mm4"].v)
                        ng = min(8, NM - 8 * gg)
                        bat_v[gg] = dma(
                            "sync", wdbat[pb][:, 0:512 * ng].rearrange("p (g n) -> p g n", g=ng),
                            Wd_dq.rearrange("(g p) dd -> p g dd", p=128)[:, 8 * gg:8 * gg + ng,
                                                                        hp0:hp0 + 512], d_wd[pb])
                    load_bat(0)
                    hv_of = {}
                    for m in range(NM):
                        if m % 8 == 0 and m // 8 + 1 <= 10:
                            load_bat(m // 8 + 1)
                        if m % 2 == 0:
                            hb2 = (43 * g2 + m // 2) % 3
                            wait("gpsimd", Z["t_mm4"], max(0, Z["t_mm4"].v - 4))
                            hv_of[m // 2] = dma(
                                "gpsimd", htb[hb2][:, 0:2048].rearrange("p (c t) -> p c t", c=2),
                                HT[128 * m:128 * (m + 2),
                                   1024 * th:1024 * (th + 1)].rearrange(
                                    "(c p) t -> p c t", p=128), d_hr[hb2])
                        hb = (43 * g2 + m // 2) % 3
                        hoff = 1024 * (m % 2)
                        gg, gi = m // 8, m % 8
                        pb = (g2 * 11 + gg) % 2
                        wait("tensor", d_wd[pb], bat_v[gg])
                        wait("tensor", d_hr[hb], hv_of[m // 2])
                        if m == 0:
                            for c4 in range(4):
                                if g2 > 0:
                                    wait("tensor", Z["a_silu4"], 4 * (g2 - 1) + c4 + 1)
                                last = (c4 == 3)
                                P.em("tensor", lambda e, gi=gi, pb=pb, hb=hb, hoff=hoff,
                                     c4=c4, last=last: [
                                    (ins.then_inc(Z["t_mm4"].h, 1) if (last and b == 1) else ins)
                                    for b in range(2)
                                    for ins in [e.matmul(
                                        psD4[c4][:, 512 * b:512 * (b + 1)],
                                        wdbat[pb][:, 512 * gi + 128 * c4:512 * gi + 128 * (c4 + 1)],
                                        htb[hb][:, hoff + 512 * b:hoff + 512 * (b + 1)],
                                        start=True, stop=False)]][-1])
                            Z["t_mm4"].v += 1
                        else:
                            P.em("tensor", lambda e, m=m, gi=gi, pb=pb, hb=hb, hoff=hoff: [
                                e.matmul(psD4[c4][:, 512 * b:512 * (b + 1)],
                                         wdbat[pb][:, 512 * gi + 128 * c4:512 * gi + 128 * (c4 + 1)],
                                         htb[hb][:, hoff + 512 * b:hoff + 512 * (b + 1)],
                                         start=False, stop=(m == NM - 1))
                                for c4 in range(4) for b in range(2)][-1].then_inc(Z["t_mm4"].h, 1))
                            Z["t_mm4"].v += 1
                    wait("scalar", Z["t_mm4"], Z["t_mm4"].v)
                    for c4 in range(4):
                        sp = c4 % 2
                        wait("scalar", d_gt[sp], d_gt[sp].v)
                        sv = inc("scalar", lambda e, c4=c4, sp=sp: e.activation(
                            out=gtstg[sp][:, :], in_=psD4[c4][:, :], func=AF.Silu),
                            Z["a_silu4"])
                        wait("scalar", Z["a_silu4"], sv)
                        dma("scalar", GT[128 * (4 * hq + c4):128 * (4 * hq + c4 + 1),
                                         1024 * th:1024 * (th + 1)], gtstg[sp][:, :], d_gt[sp])

        ctxw_f = ps[:, 3072:3584]

        trp5 = [ps[:, 0:64].bitcast(BF16), ps[:, 512:576].bitcast(BF16)]

        def gTl(pb):
            return SB[:, 4096 * pb:4096 * (pb + 1)]

        def sg5(pb):
            return SB[:, 8192 + 4096 * pb:8192 + 4096 * (pb + 1)]

        qkwB = [qkw, qstg[0]]
        kkwB = [kkw, qstg[1]]
        vpwB = [vpw, siluG[0]]
        vkwB = [vkw, siluG[1]]
        PTwB = [PTw, hstg[0]]
        scwB = [ps[:, 2048:3072], ps[:, 1024:2048]]

        def emit_ab_scores(s, bb, nb, bank_lv):
            w = nb % 2
            wait("gpsimd", Z["d_bank"], bank_lv)
            wait("gpsimd", Z["t_sc"], max(0, nb - 1))
            for (dst, srcf) in ((qkwB[w], QTb), (kkwB[w], KTb)):
                inc("gpsimd", lambda e, dst=dst, srcf=srcf, s=s, bb=bb: e.tensor_copy(
                    out=dst[:, :].rearrange("p (a b) -> p a b", a=32, b=32),
                    in_=srcf(s).rearrange("p (b t) -> p t b", b=32)[:, 32 * bb:32 * (bb + 1), :]),
                    Z["g_stk"])
            wait("scalar", Z["d_bank"], bank_lv)
            wait("scalar", Z["t_vt"], max(0, nb - 1))
            inc("scalar", lambda e, s=s, bb=bb, w=w: e.copy(
                out=vpwB[w][:, :].rearrange("p (a b) -> p a b", a=32, b=32),
                in_=VTb(s).rearrange("p (b t) -> p t b", b=32)[:, 32 * bb:32 * (bb + 1), :]),
                Z["a_vst"])
            wait("tensor", Z["g_stk"], Z["g_stk"].v)
            wait("tensor", Z["a_exp"], max(0, nb - 1))
            P.em("tensor", lambda e, w=w: [
                (e.matmul(scwB[w][:, 128 * gg:128 * (gg + 1)],
                          qkwB[w][:, 128 * gg:128 * (gg + 1)],
                          kkwB[w][:, 128 * gg:128 * (gg + 1)], start=True, stop=False),
                 e.matmul(scwB[w][:, 128 * gg:128 * (gg + 1)], mskl[:, :], mskr[:, :],
                          start=False, stop=True))
                for gg in range(8)][-1][-1].then_inc(Z["t_sc"].h, 1))
            Z["t_sc"].v += 1
            wait("scalar", Z["t_sc"], Z["t_sc"].v)
            wait("scalar", Z["v_p3"], max(0, 4 * nb - 1))
            ev = inc("scalar", lambda e, w=w: e.activation(
                out=Ew[:, :], in_=scwB[w][:, :], func=AF.Exp, bias=0.0, scale=1.0), Z["a_exp"])
            wait("vector", Z["a_exp"], ev)
            rv = inc("vector", lambda e: e.reduce_sum(
                out=sumw[:, :].rearrange("p (g o) -> p g o", o=1),
                in_=Ew[:, :].rearrange("p (g t) -> p g t", g=8), axis=AX), Z["v_p3"])
            wait("vector", Z["v_p3"], rv)
            cv = inc("vector", lambda e: e.reciprocal(out=rcpw[:, :], in_=sumw[:, :]), Z["v_p3"])
            wait("vector", Z["v_p3"], cv)
            mv = inc("vector", lambda e: e.tensor_tensor(
                out=Pw[:, :].rearrange("p (g t) -> p g t", g=8),
                in0=Ew[:, :].rearrange("p (g t) -> p g t", g=8),
                in1=rcpw[:, :].rearrange("p (g o) -> p g o", o=1).broadcast_to([128, 8, 128]),
                op=ALU.mult), Z["v_p3"])
            wait("vector", Z["v_p3"], mv)
            wait("vector", Z["t_cx"], max(0, 2 * (nb - 1)))
            inc("vector", lambda e, w=w: e.transpose(out=PTwB[w][:, :], in_=Pw[:, :]), Z["v_p3"])

        def emit_ab_vtr(s, bb, nb):
            w = nb % 2
            wait("tensor", Z["a_vk"], nb)
            wait("tensor", Z["a_vst"], Z["a_vst"].v)
            P.em("tensor", lambda e, w=w: [
                e.transpose(vpsw[:, 128 * gg:128 * (gg + 1)],
                            vpwB[w][:, 128 * gg:128 * (gg + 1)], id_b[:, :])
                for gg in range(8)][-1].then_inc(Z["t_vt"].h, 1))
            Z["t_vt"].v += 1
            wait("scalar", Z["t_vt"], Z["t_vt"].v)
            wait("scalar", Z["t_cx"], max(0, 2 * (nb - 1)))
            inc("scalar", lambda e, w=w: e.copy(out=vkwB[w][:, :], in_=vpsw[:, :]), Z["a_vk"])

        def emit_ab_ctx(s, bb, nb):
            w = nb % 2
            wait("tensor", Z["a_vk"], nb + 1)
            wait("tensor", Z["v_p3"], 4 * (nb + 1))
            for h2 in range(2):
                wait("tensor", Z["v_dep"], 2 * nb + h2)
                P.em("tensor", lambda e, h2=h2, w=w: [
                    e.matmul(ctxw_f[:, 128 * g4:128 * (g4 + 1)],
                             vkwB[w][:, 128 * (4 * h2 + g4):128 * (4 * h2 + g4 + 1)],
                             PTwB[w][:, 128 * (4 * h2 + g4):128 * (4 * h2 + g4 + 1)],
                             start=True, stop=True)
                    for g4 in range(4)][-1].then_inc(Z["t_cx"].h, 1))
                Z["t_cx"].v += 1
                wait("vector", Z["t_cx"], Z["t_cx"].v)
                inc("vector", lambda e, s=s, bb=bb, h2=h2: e.tensor_copy(
                    out=sTb(s).rearrange("p (b t) -> p t b", b=32)[:, 32 * bb + 16 * h2:
                                                                   32 * bb + 16 * h2 + 16, :],
                    in_=ctxw_f[:, :].rearrange("p (t b) -> p t b", t=16, b=32)), Z["v_dep"])

        def emit_p5_tile(ti):
            pb = ti % 2
            if ti >= 2:
                wait("sync", Z["t_t5"], Z["t_t5"].v)
                wait("sync", Z["v_m5"], max(0, Z["v_m5"].v - 1))
            lv = dma("sync", gTl(pb)[:, :].rearrange("p (c t) -> p c t", c=32),
                     GT.rearrange("(c p) t -> p c t", p=128)[:, :, 128 * ti:128 * (ti + 1)],
                     Z["d_bank"])
            wait("sync", Z["v_a5"], max(0, Z["v_a5"].v - 1))
            xlv = dma("sync", xbuf[pb][:, :], x_in[128 * ti:128 * (ti + 1), :], d_x[pb])
            wait("vector", Z["d_bank"], lv)
            wait("vector", Z["v_dep"], 8 * (ti + 1))
            mv = inc("vector", lambda e, pb=pb, ti=ti: e.tensor_tensor(
                out=sg5(pb)[:, :], in0=sTb(ti % 2)[:, :], in1=gTl(pb)[:, :], op=ALU.mult),
                Z["v_m5"])
            wait("tensor", Z["v_m5"], mv)
            if ti >= 2:
                wait("scalar", Z["d_out"], Z["d_out"].v)
            if ti == 0:
                wait("scalar", Z["d_dqo0"], Z["d_dqo0"].v)
                wait("scalar", Z["d_dqo1"], Z["d_dqo1"].v)
            for hh in range(H):
                tb = hh % 2
                wait("tensor", Z["a_e5"], max(0, Z["a_e5"].v - 1))
                inc("tensor", lambda e, hh=hh, pb=pb, tb=tb: e.transpose(
                    trp5[tb][:, :], sg5(pb)[:, 128 * hh:128 * (hh + 1)], id_b[:, :]),
                    Z["t_t5"])
                wait("scalar", Z["t_t5"], Z["t_t5"].v)
                inc("scalar", lambda e, hh=hh, pb=pb, tb=tb: e.copy(
                    out=outb[pb][:, 128 * hh:128 * (hh + 1)], in_=trp5[tb][:, :]), Z["a_e5"])
            wait("gpsimd", Z["a_e5"], Z["a_e5"].v)
            wait("gpsimd", d_x[pb], xlv)
            av = inc("gpsimd", lambda e, pb=pb: e.tensor_tensor(
                out=outb[pb][:, :], in0=xbuf[pb][:, :], in1=outb[pb][:, :],
                op=ALU.add), Z["v_a5"])
            wait("scalar", Z["v_a5"], av)
            dma("scalar", out_t[128 * ti:128 * (ti + 1), :], outb[pb][:, :], Z["d_out"])

        def emit_p3p5():
            wait("sync", Z["t_mm4"], Z["t_mm4"].v)
            wait("sync", Z["d_qkv0"], Z["d_qkv0"].v)
            wait("sync", Z["d_qkv1"], Z["d_qkv1"].v)
            wait("sync", Z["d_gt0"], Z["d_gt0"].v)
            wait("sync", Z["d_gt1"], Z["d_gt1"].v)
            wait("tensor", Z["a_silu4"], Z["a_silu4"].v)
            blv = {0: emit_bank_load(0, first=True)}
            blv[1] = emit_bank_load(1, first=True)
            nb = 0
            for sb in range(16):
                s = sb % 2
                if sb >= 1 and sb + 1 <= 15:
                    blv[sb + 1] = emit_bank_load(sb + 1)
                for bb in range(4):
                    if nb >= 1:
                        pbb = (nb - 1) % 4
                        ps_ = ((nb - 1) // 4) % 2
                        emit_ab_ctx(ps_, pbb, nb - 1)
                    emit_ab_scores(s, bb, nb, blv[sb])
                    emit_ab_vtr(s, bb, nb)
                    nb += 1
                if sb >= 1 and not skip_p5:
                    emit_p5_tile(sb - 1)
            emit_ab_ctx((15) % 2, 3, nb - 1)
            if not skip_p5:
                emit_p5_tile(15)
                wait("scalar", Z["d_out"], Z["d_out"].v)
            else:
                wait("scalar", Z["v_dep"], Z["v_dep"].v)
                for sq in range(2):
                    dma("scalar", sT_h[:, 4096 * sq:4096 * (sq + 1)], sTb(sq)[:, :], Z["d_stw"])
                wait("scalar", Z["d_stw"], Z["d_stw"].v)
                for ti in range(16):
                    dma("scalar", out_t[128 * ti:128 * (ti + 1), :], R2[:, 0:4096], Z["d_out"])
                wait("scalar", Z["d_out"], Z["d_out"].v)

        emit_p0c_half(0)
        sl_done = emit_dequant_gateup()
        emit_p1_half(0)
        emit_p2_half(0, sl_done)
        emit_p0c_half(1)
        emit_p1_half(1)
        emit_p2_half(1, sl_done)
        emit_p4()
        if skip_tail:
            wait("scalar", Z["d_gt0"], Z["d_gt0"].v)
            wait("scalar", Z["d_gt1"], Z["d_gt1"].v)
            for ti in range(16):
                dma("scalar", out_t[128 * ti:128 * (ti + 1), :], R2[:, 0:4096], Z["d_out"])
            wait("scalar", Z["d_out"], Z["d_out"].v)
        else:
            emit_p3p5()

        block = S.enter_context(nc.Block())

        @block.sync
        def _(e):
            for f in P.ops["sync"]:
                f(e)

        @block.scalar
        def _(e):
            for f in P.ops["scalar"]:
                f(e)

        @block.vector
        def _(e):
            for f in P.ops["vector"]:
                f(e)

        @block.tensor
        def _(e):
            for f in P.ops["tensor"]:
                f(e)

        @block.gpsimd
        def _(e):
            for f in P.ops["gpsimd"]:
                f(e)

    return nc


def _consts():
    identb = np.eye(128, dtype=ml_dtypes.bfloat16)
    identf = np.eye(128, dtype=np.float32)
    BIG = np.float32(32768.0)
    ml_ = np.zeros((5, 128), dtype=np.float32)
    mr_ = np.zeros((5, 128), dtype=np.float32)
    ml_[0, :] = 1.0
    mr_[0, :] = -BIG
    for mm in range(4):
        ml_[1 + mm, 32 * mm:32 * (mm + 1)] = 1.0
        mr_[1 + mm, 32 * mm:32 * (mm + 1)] = BIG
    maskl = ml_.astype(ml_dtypes.bfloat16)
    maskr = mr_.astype(ml_dtypes.bfloat16)
    shift8 = (4 * (np.arange(128, dtype=np.int32) % 8)).reshape(128, 1)
    return identb, identf, maskl, maskr, shift8


def make_inputs(inputs):
    identb, identf, maskl, maskr, shift8 = _consts()
    ca = np.ascontiguousarray
    common = {
        "wq": ca(np.asarray(inputs["L0_attn_wq"], np.float32)),
        "wk": ca(np.asarray(inputs["L0_attn_wk"], np.float32)),
        "wv": ca(np.asarray(inputs["L0_attn_wv"], np.float32)),
        "bq": ca(np.asarray(inputs["L0_attn_bq"], np.float32)),
        "bk": ca(np.asarray(inputs["L0_attn_bk"], np.float32)),
        "bv": ca(np.asarray(inputs["L0_attn_bv"], np.float32)),
        "qwg": ca(np.asarray(inputs["L0_ffn_gate_qweight"], np.int32)),
        "scg": ca(np.asarray(inputs["L0_ffn_gate_scales"], np.float32)),
        "qzg": ca(np.asarray(inputs["L0_ffn_gate_qzeros"], np.int32)),
        "qwu": ca(np.asarray(inputs["L0_ffn_up_qweight"], np.int32)),
        "scu": ca(np.asarray(inputs["L0_ffn_up_scales"], np.float32)),
        "qzu": ca(np.asarray(inputs["L0_ffn_up_qzeros"], np.int32)),
        "qwd": ca(np.asarray(inputs["L0_ffn_down_qweight"], np.int32)),
        "scd": ca(np.asarray(inputs["L0_ffn_down_scales"], np.float32)),
        "qzd": ca(np.asarray(inputs["L0_ffn_down_qzeros"], np.int32)),
        "identb": identb, "identf": identf, "maskl": maskl, "maskr": maskr, "shift8": shift8,
    }
    x = np.asarray(inputs["L0_attn_hidden_states"], dtype=np.float32).reshape(16384, D)
    in_maps = []
    for c in range(NCORES):
        m = dict(common)
        m["x"] = np.ascontiguousarray(x[T * c:T * (c + 1)])
        in_maps.append(m)
    return in_maps


def kernel(**inputs):
    in_maps = make_inputs(inputs)
    nc = build_nc(debug=False)
    res = run_bass_kernel_spmd(nc, in_maps, core_ids=list(range(NCORES)))
    out = np.concatenate([np.asarray(r["out"], np.float32) for r in res.results], axis=0)
    return out.reshape(2, 8192, D)


# version 10 (final)
# speedup vs baseline: 1.0470x; 1.0470x over previous
import contextlib
import math
import numpy as np
import ml_dtypes
import concourse.bass as bass
import concourse.mybir as mybir
from concourse.bass_utils import run_bass_kernel_spmd

F32 = mybir.dt.float32
BF16 = mybir.dt.bfloat16
I32 = mybir.dt.int32
AF = mybir.ActivationFunctionType
ALU = mybir.AluOpType
AX = mybir.AxisListType.X

NCORES = 8
T = 2048
TH = 1024
D = 4096
H = 32
FF = 11008
NKC = 32
NM = 86
QSCALE = 1.0 / math.sqrt(128.0)
EPS = 1e-5
ENGS = ("sync", "scalar", "vector", "tensor", "gpsimd")


class Plan:
    def __init__(self):
        self.ops = {e: [] for e in ENGS}

    def em(self, eng, fn):
        self.ops[eng].append(fn)


class Sem:
    __slots__ = ("h", "v", "name")

    def __init__(self, h, name):
        self.h = h
        self.v = 0
        self.name = name


def build_nc(debug=False, skip_tail=False, skip_p5=False):
    nc = bass.Bass("TRN2", target_bir_lowering=False)
    P = Plan()

    x_in = nc.dram_tensor("x", [T, D], F32, kind="ExternalInput").ap()
    w_in = {w: nc.dram_tensor(f"w{w}", [D, D], F32, kind="ExternalInput").ap() for w in "qkv"}
    b_in = {w: nc.dram_tensor(f"b{w}", [D], F32, kind="ExternalInput").ap() for w in "qkv"}
    qwg = nc.dram_tensor("qwg", [512, FF], I32, kind="ExternalInput").ap()
    scg = nc.dram_tensor("scg", [32, FF], F32, kind="ExternalInput").ap()
    qzg = nc.dram_tensor("qzg", [32, FF // 8], I32, kind="ExternalInput").ap()
    qwu = nc.dram_tensor("qwu", [512, FF], I32, kind="ExternalInput").ap()
    scu = nc.dram_tensor("scu", [32, FF], F32, kind="ExternalInput").ap()
    qzu = nc.dram_tensor("qzu", [32, FF // 8], I32, kind="ExternalInput").ap()
    qwd = nc.dram_tensor("qwd", [FF // 8, D], I32, kind="ExternalInput").ap()
    scd = nc.dram_tensor("scd", [NM, D], F32, kind="ExternalInput").ap()
    qzd = nc.dram_tensor("qzd", [NM, D // 8], I32, kind="ExternalInput").ap()
    identb_in = nc.dram_tensor("identb", [128, 128], BF16, kind="ExternalInput").ap()
    identf_in = nc.dram_tensor("identf", [128, 128], F32, kind="ExternalInput").ap()
    maskl_in = nc.dram_tensor("maskl", [5, 128], BF16, kind="ExternalInput").ap()
    maskr_in = nc.dram_tensor("maskr", [5, 128], BF16, kind="ExternalInput").ap()
    shift8_in = nc.dram_tensor("shift8", [128, 1], I32, kind="ExternalInput").ap()
    out_t = nc.dram_tensor("out", [T, D], F32, kind="ExternalOutput").ap()

    sk = "ExternalOutput" if debug else "Internal"
    wbf = {w: nc.dram_tensor(f"wbf_{w}", [D, D], BF16, kind="Internal").ap() for w in "qkv"}
    wT = {w: nc.dram_tensor(f"wT_{w}", [D, D], BF16, kind=sk if w == "q" else "Internal").ap() for w in "qkv"}
    Wg_dq = nc.dram_tensor("Wg_dq", [D, FF], BF16, kind=sk).ap()
    Wu_dq = nc.dram_tensor("Wu_dq", [D, FF], BF16, kind="Internal").ap()
    Wd_dq = nc.dram_tensor("Wd_dq", [FF, D], BF16, kind=sk).ap()
    QT = nc.dram_tensor("QT", [D, T], BF16, kind=sk).ap()
    KT = nc.dram_tensor("KT", [D, T], BF16, kind=sk).ap()
    VT = nc.dram_tensor("VT", [D, T], BF16, kind=sk).ap()
    HT = nc.dram_tensor("HT", [FF, T], BF16, kind=sk).ap()
    GT = nc.dram_tensor("GT", [D, T], BF16, kind=sk).ap()
    sT_h = nc.dram_tensor("sT_h", [128, 16 * D], BF16, kind=sk).ap()
    z_n = {"g": nc.dram_tensor("zg_n", [32, FF], F32, kind="Internal").ap(),
           "u": nc.dram_tensor("zu_n", [32, FF], F32, kind="Internal").ap(),
           "d": nc.dram_tensor("zd_n", [NM, D], F32, kind="Internal").ap()}
    xnT_dbg = nc.dram_tensor("xnT_dbg", [D, T], BF16, kind=sk).ap() if debug else None
    sT_dbg = nc.dram_tensor("sT_dbg", [128, 65536], BF16, kind=sk).ap() if debug else None

    with contextlib.ExitStack() as S:
        R1 = S.enter_context(nc.sbuf_tensor("R1", [128, 32768], BF16))
        R2 = S.enter_context(nc.sbuf_tensor("R2", [128, 16384], F32))
        SB = S.enter_context(nc.sbuf_tensor("SB", [128, 16384], BF16))
        ps = S.enter_context(nc.psum_tensor("ps", [128, 4096], F32))

        def xnT(c):
            return R1[:, 1024 * c:1024 * (c + 1)]

        QTs = R1[:, 0:8192]
        KTs = R1[:, 8192:16384]
        VTs = R1[:, 16384:24576]
        sT = R1[:, 24576:32768]
        sgT = QTs

        xbuf = [R2[:, 0:4096], R2[:, 4096:8192]]
        xsq = R2[:, 8192:12288]
        xnb = [R2[:, 12288:14336].bitcast(BF16), R2[:, 14336:16384].bitcast(BF16)]
        z4t = [SB[:, 5504:6192].bitcast(I32), SB[:, 6192:6880].bitcast(I32)]
        znat = SB[:, 0:5504].bitcast(F32)
        dq_qw = R2[:, 0:2752].bitcast(I32)
        dq_w4 = R2[:, 2752:5504].bitcast(I32)
        dq_z = R2[:, 5504:6880].bitcast(BF16)
        dq_s = R2[:, 6880:8256].bitcast(BF16)
        dq_T = [R2[:, 8256:9632].bitcast(BF16), R2[:, 9632:11008].bitcast(BF16)]
        dd_qw = R2[:, 0:4096].bitcast(I32)
        dd_w4 = R2[:, 4096:8192].bitcast(I32)
        dd_z = R2[:, 8192:10240].bitcast(BF16)
        dd_s = R2[:, 10240:12288].bitcast(BF16)
        dd_T = [R2[:, 12288:14336].bitcast(BF16), R2[:, 14336:16384].bitcast(BF16)]
        outb = [R2[:, 8192:12288], R2[:, 12288:16384]]

        wtr = [SB[:, 12288:16384]]
        wstat = [SB[:, 0:4096], SB[:, 4096:8192], SB[:, 8192:12288]]
        gstat = [SB[:, 0:4096], SB[:, 4096:8192]]
        ustat = [SB[:, 8192:12288], SB[:, 12288:16384]]
        wdbat = [SB[:, 0:4096], SB[:, 4096:8192]]
        htb = [SB[:, 8192:10240], SB[:, 10240:12288], SB[:, 12288:14336]]
        gtstg = [SB[:, 14336:15360], SB[:, 15360:16384]]

        qstg = [S.enter_context(nc.sbuf_tensor(f"qstg{i}", [128, 1024], BF16)) for i in range(2)]
        siluG = [S.enter_context(nc.sbuf_tensor(f"siluG{i}", [128, 1024], BF16)) for i in range(2)]
        hstg = [S.enter_context(nc.sbuf_tensor(f"hstg{i}", [128, 1024], BF16)) for i in range(2)]
        id_b = S.enter_context(nc.sbuf_tensor("id_b", [128, 128], BF16))
        id_f = S.enter_context(nc.sbuf_tensor("id_f", [128, 128], F32))
        mskl = S.enter_context(nc.sbuf_tensor("mskl", [5, 128], BF16))
        mskr = S.enter_context(nc.sbuf_tensor("mskr", [5, 128], BF16))
        sh8 = S.enter_context(nc.sbuf_tensor("sh8", [128, 1], I32))
        braw = S.enter_context(nc.sbuf_tensor("braw", [32, 128], F32))
        bt = {w: S.enter_context(nc.sbuf_tensor(f"bt_{w}", [128, 32], F32)) for w in "qkv"}
        qzs = S.enter_context(nc.sbuf_tensor("qzs", [NM, FF // 8], I32))
        stt2 = [{n: S.enter_context(nc.sbuf_tensor(f"st_{n}{i}", [128, 1], F32))
                 for n in ("s1", "s2", "mu", "m2", "mu2", "ve", "ln", "rs", "nm")}
                for i in range(2)]
        qkw = S.enter_context(nc.sbuf_tensor("qkw", [128, 1024], BF16))
        kkw = S.enter_context(nc.sbuf_tensor("kkw", [128, 1024], BF16))
        vpw = S.enter_context(nc.sbuf_tensor("vpw", [128, 1024], BF16))
        vkw = S.enter_context(nc.sbuf_tensor("vkw", [128, 1024], BF16))
        Ew = S.enter_context(nc.sbuf_tensor("Ew", [128, 1024], F32))
        Pw = S.enter_context(nc.sbuf_tensor("Pw", [128, 1024], BF16))
        PTw = S.enter_context(nc.sbuf_tensor("PTw", [128, 1024], BF16))
        sumw = S.enter_context(nc.sbuf_tensor("sumw", [128, 8], F32))
        rcpw = S.enter_context(nc.sbuf_tensor("rcpw", [128, 8], F32))
        qstk = [S.enter_context(nc.sbuf_tensor(f"qstk{i}", [128, 128], BF16)) for i in range(2)]
        kstk = [S.enter_context(nc.sbuf_tensor(f"kstk{i}", [128, 128], BF16)) for i in range(2)]
        vpre = [S.enter_context(nc.sbuf_tensor(f"vpre{i}", [128, 128], BF16)) for i in range(2)]
        vstk = [S.enter_context(nc.sbuf_tensor(f"vstk{i}", [128, 128], BF16)) for i in range(2)]
        Esb = [S.enter_context(nc.sbuf_tensor(f"Esb{i}", [128, 128], F32)) for i in range(2)]
        Pbd = [S.enter_context(nc.sbuf_tensor(f"Pbd{i}", [128, 128], BF16)) for i in range(2)]
        PTbd = [S.enter_context(nc.sbuf_tensor(f"PTbd{i}", [128, 128], BF16)) for i in range(2)]
        nmax = [S.enter_context(nc.sbuf_tensor(f"nmax{i}", [128, 1], F32)) for i in range(2)]
        ssum = [S.enter_context(nc.sbuf_tensor(f"ssum{i}", [128, 1], F32)) for i in range(2)]
        rcp = [S.enter_context(nc.sbuf_tensor(f"rcp{i}", [128, 1], F32)) for i in range(2)]

        trp = [ps[:, 3072:3136].bitcast(BF16), ps[:, 3584:3648].bitcast(BF16)]
        trpf = ps[:, 3072:3104]
        psQ4 = [ps[:, 0:1024], ps[:, 1024:2048], ps[:, 2048:3072], ps[:, 3072:4096]]
        psG = [ps[:, 0:1024], ps[:, 2048:3072]]
        psU = [ps[:, 1024:2048], ps[:, 3072:4096]]
        psD = [ps[:, 0:2048], ps[:, 2048:4096]]
        ps_s = [ps[:, 0:128], ps[:, 1536:1664]]
        vps = [ps[:, 512:576].bitcast(BF16), ps[:, 2048:2112].bitcast(BF16)]
        ps_c = [ps[:, 1024:1152], ps[:, 2560:2688]]

        names = ["d_small", "d_wcast", "d_wtr0", "d_wtr1", "d_wsc0", "d_wsc1",
                 "d_x0", "d_x1", "d_stat0", "d_stat1", "d_stat2", "d_qw0", "d_qw1",
                 "d_rep0", "d_rep1", "d_dqo0", "d_dqo1", "d_qkv0", "d_qkv1",
                 "d_sg0", "d_sg1", "d_su0", "d_su1", "d_ht_w", "d_hr0", "d_hr1", "d_hr2",
                 "d_wd0", "d_wd1", "d_gt0", "d_gt1", "d_bank", "d_out", "d_zw", "d_stw",
                 "v_misc", "v_ln", "v_ev", "v_dq", "v_hm", "v_dep", "v_p3", "v_m5", "v_a5",
                 "a_misc", "a_ln", "a_xn", "a_ev1", "a_silu", "a_silu4", "a_exp", "a_pm",
                 "a_vk", "a_e5", "t_misc", "t_xp", "t_mm1", "t_mm2", "t_mm4", "t_sc",
                 "t_vt", "t_cx", "t_t5", "g_stk", "a_vst"]
        Z = {n: Sem(S.enter_context(nc.semaphore(n)), n) for n in names}

        def dma(q, out, in_, sem, **kw):
            sem.v += 16
            P.em(q, lambda e, out=out, in_=in_, sem=sem, kw=kw:
                 e.dma_start(out=out, in_=in_, **kw).then_inc(sem.h, 16))
            return sem.v

        def wait(eng, sem, v):
            if v > 0:
                P.em(eng, lambda e, sem=sem, v=v: e.wait_ge(sem.h, v))

        def inc(eng, fn, sem, n=1):
            sem.v += n
            P.em(eng, lambda e, fn=fn, sem=sem, n=n: fn(e).then_inc(sem.h, n))
            return sem.v

        def op(eng, fn):
            P.em(eng, lambda e, fn=fn: fn(e))

        for cc in range(4):
            dma("gpsimd", wbf["q"][:, 1024 * cc:1024 * (cc + 1)],
                w_in["q"][:, 1024 * cc:1024 * (cc + 1)], Z["d_wcast"])
        dma("sync", id_b[:, :], identb_in[:, :], Z["d_small"])
        dma("sync", id_f[:, :], identf_in[:, :], Z["d_small"])
        dma("sync", mskl[:, :], maskl_in[:, :], Z["d_small"])
        dma("sync", mskr[:, :], maskr_in[:, :], Z["d_small"])
        dma("sync", sh8[:, :], shift8_in[:, :], Z["d_small"])
        tv_prev = 0
        for w, scale in (("q", QSCALE), ("k", 1.0), ("v", 1.0)):
            wait("sync", Z["t_misc"], tv_prev)
            lv = dma("sync", braw[:, :], b_in[w].rearrange("(g p) -> g p", p=128), Z["d_small"])
            wait("tensor", Z["d_small"], lv)
            tv_prev = inc("tensor", lambda e: e.transpose(trpf[:, :], braw[:, :], id_f[0:32, 0:32]), Z["t_misc"])
            wait("scalar", Z["t_misc"], tv_prev)
            av = inc("scalar", lambda e, w=w, scale=scale: e.activation(
                out=bt[w][:, :], in_=trpf[:, :], func=AF.Identity, bias=0.0, scale=float(scale)),
                Z["a_misc"])
            wait("tensor", Z["a_misc"], av)

        for key, qz_src, G, NN in (("g", qzg, 32, FF), ("u", qzu, 32, FF), ("d", qzd, NM, D)):
            wait("sync", Z["v_misc"], Z["v_misc"].v)
            lv = dma("sync", qzs[0:G, 0:NN // 8], qz_src[:, :], Z["d_small"])
            wait("vector", Z["d_small"], lv)
            NS = 344 if NN == FF else 128
            nsl = (NN // 8) // NS
            for s in range(nsl):
                wait("vector", Z["d_zw"], Z["d_zw"].v)
                for rb in range(8):
                    pb = rb % 2
                    wait("vector", Z["v_misc"], max(0, Z["v_misc"].v - 2))
                    uv = inc("vector", lambda e, s=s, rb=rb, pb=pb, G=G, NS=NS: e.tensor_scalar(
                        out=z4t[pb][0:G, 0:NS], in0=qzs[0:G, NS * s:NS * (s + 1)],
                        scalar1=4 * rb, scalar2=15, op0=ALU.logical_shift_right, op1=ALU.bitwise_and),
                        Z["v_misc"])
                    wait("vector", Z["v_misc"], uv)
                    inc("vector", lambda e, s=s, rb=rb, pb=pb, G=G, NS=NS: e.tensor_scalar(
                        out=znat[0:G, 0:8 * NS].rearrange("g (n e) -> g n e", e=8)[:, :, rb:rb + 1],
                        in0=z4t[pb][0:G, 0:NS].rearrange("g (n o) -> g n o", o=1),
                        scalar1=1, scalar2=None, op0=ALU.add), Z["v_misc"])
                wait("gpsimd", Z["v_misc"], Z["v_misc"].v)
                dma("gpsimd", z_n[key][0:G, 8 * NS * s:8 * NS * (s + 1)], znat[0:G, 0:8 * NS], Z["d_zw"])
        zw_all = Z["d_zw"].v
        vmisc_all = Z["v_misc"].v

        for w in "kv":
            for cc in range(4):
                dma("gpsimd", wbf[w][:, 1024 * cc:1024 * (cc + 1)],
                    w_in[w][:, 1024 * cc:1024 * (cc + 1)], Z["d_wcast"])
        d_tr = [Z["d_wtr0"], Z["d_wtr1"]]
        d_sc = [Z["d_wsc0"], Z["d_wsc1"]]
        _wpair_state = {"w": None, "trv": [0, 0]}

        _trv = {}
        NWTR = len(wtr)

        def emit_wtr(w, a):
            base = {"q": 0, "k": 4, "v": 8}[w]
            wait("sync", Z["d_wcast"], 16 * (base + a // 8 + 1))
            pb = a % NWTR
            wait("sync", d_sc[pb], d_sc[pb].v)
            dma("sync", wtr[pb][:, :], wbf[w][:, 128 * a:128 * (a + 1)], d_tr[pb], transpose=True)
            _trv[(w, a)] = d_tr[pb].v

        def emit_wsc(w, a):
            pb = a % NWTR
            wait("sync", d_tr[pb], _trv[(w, a)])
            base_s = 128 * (a // 8) + 16 * (a % 8)
            dst = wT[w].rearrange("(kb s) h -> s kb h", kb=8)[base_s:base_s + 16, :, :]
            dma("sync", dst, wtr[pb][:, :], d_sc[pb])

        def emit_wpair(w, a, q=None):
            emit_wtr(w, a)
            emit_wsc(w, a)

        def emit_wpairs_drain():
            wait("sync", d_sc[0], d_sc[0].v)
            wait("sync", d_sc[1], d_sc[1].v)

        d_x = [Z["d_x0"], Z["d_x1"]]

        def emit_p0c_half(h):
            if h == 1:
                wait("scalar", Z["v_dq"], Z["v_dq"].v)
                wait("scalar", Z["d_dqo0"], Z["d_dqo0"].v)
                wait("scalar", Z["d_dqo1"], Z["d_dqo1"].v)
                wait("tensor", Z["v_hm"], Z["v_hm"].v)
                wait("vector", Z["t_mm2"], Z["t_mm2"].v)
            xv = [0, 0]
            for i in range(8):
                ti = 8 * h + i
                pb = i % 2
                if i < 2:
                    xv[pb] = dma("scalar", xbuf[pb][:, :], x_in[128 * ti:128 * (ti + 1), :], d_x[pb])
                wait("scalar", d_x[pb], xv[pb])
                stt = stt2[pb]
                sqv = inc("scalar", lambda e, pb=pb: e.activation(
                    out=xsq[:, :], in_=xbuf[pb][:, :], func=AF.Square), Z["a_ln"])
                wait("vector", Z["a_ln"], sqv)
                wait("vector", Z["a_xn"], max(0, Z["a_xn"].v - 1))
                op("vector", lambda e: e.reduce_sum(out=stt["s2"][:, :], in_=xsq[:, :], axis=AX))
                r1v = inc("vector", lambda e, pb=pb: e.reduce_sum(
                    out=stt["s1"][:, :], in_=xbuf[pb][:, :], axis=AX), Z["v_ln"])
                wait("vector", Z["v_ln"], r1v)
                op("vector", lambda e: e.tensor_scalar(
                    out=stt["mu"][:, :], in0=stt["s1"][:, :], scalar1=1.0 / D, scalar2=None, op0=ALU.mult))
                m2v = inc("vector", lambda e: e.tensor_scalar(
                    out=stt["m2"][:, :], in0=stt["s2"][:, :], scalar1=1.0 / D, scalar2=None, op0=ALU.mult),
                    Z["v_ln"])
                wait("vector", Z["v_ln"], m2v)
                mu2v = inc("vector", lambda e: e.tensor_tensor(
                    out=stt["mu2"][:, :], in0=stt["mu"][:, :], in1=stt["mu"][:, :], op=ALU.mult), Z["v_ln"])
                wait("vector", Z["v_ln"], mu2v)
                vev = inc("vector", lambda e: e.scalar_tensor_tensor(
                    out=stt["ve"][:, :], in0=stt["m2"][:, :], scalar=EPS, in1=stt["mu2"][:, :],
                    op0=ALU.add, op1=ALU.subtract), Z["v_ln"])
                wait("scalar", Z["v_ln"], vev)
                lnv = inc("scalar", lambda e: e.activation(
                    out=stt["ln"][:, :], in_=stt["ve"][:, :], func=AF.Ln), Z["a_ln"])
                wait("scalar", Z["a_ln"], lnv)
                rsv = inc("scalar", lambda e: e.activation(
                    out=stt["rs"][:, :], in_=stt["ln"][:, :], func=AF.Exp, bias=0.0, scale=-0.5), Z["a_ln"])
                wait("vector", Z["a_ln"], rsv)
                nmv = inc("vector", lambda e: e.scalar_tensor_tensor(
                    out=stt["nm"][:, :], in0=stt["mu"][:, :], scalar=-1.0, in1=stt["rs"][:, :],
                    op0=ALU.mult, op1=ALU.mult), Z["v_ln"])
                wait("scalar", Z["v_ln"], nmv)
                wait("scalar", Z["t_xp"], max(0, Z["t_xp"].v - NKC))
                xnv = inc("scalar", lambda e, pb=pb: e.activation(
                    out=xnb[pb][:, :], in_=xbuf[pb][:, :], func=AF.Identity,
                    bias=stt["nm"][:, 0:1], scale=stt["rs"][:, 0:1]), Z["a_xn"])
                if i + 2 < 8:
                    wait("scalar", Z["a_xn"], xnv)
                    xv[pb] = dma("scalar", xbuf[pb][:, :], x_in[128 * (ti + 2):128 * (ti + 3), :], d_x[pb])
                wait("tensor", Z["a_xn"], xnv)
                for c in range(NKC):
                    kbit, kblk = c // 4, c % 4
                    tb = c % 2
                    wait("tensor", Z["v_ev"], max(0, Z["v_ev"].v - 1))
                    src = xnb[pb][:, :].rearrange("p (a r e) -> p e a r", a=4, e=8)[:, kbit, kblk, :]
                    tpv = inc("tensor", lambda e, src=src, tb=tb: e.transpose(
                        trp[tb][:, :], src, id_b[:, :]), Z["t_xp"])
                    wait("vector", Z["t_xp"], tpv)
                    inc("vector", lambda e, c=c, i=i, tb=tb: e.tensor_copy(
                        out=xnT(c)[:, 128 * i:128 * (i + 1)], in_=trp[tb][:, :]), Z["v_ev"])
            if debug:
                wait("gpsimd", Z["v_ev"], Z["v_ev"].v)
                for c in range(NKC):
                    dma("gpsimd", xnT_dbg[128 * c:128 * (c + 1), TH * h:TH * (h + 1)],
                        xnT(c)[:, :], Z["d_zw"])

        d_qw = [Z["d_qw0"], Z["d_qw1"]]
        d_rep = [Z["d_rep0"], Z["d_rep1"]]
        d_dqo = [Z["d_dqo0"], Z["d_dqo1"]]
        SLW = 2752
        dqst = {"first": True, "sl_done": {}, "qv": 0, "rv": 0, "lb": 0}

        def emit_dq_gu_part(idx, half):
            sl, r = idx // 8, idx % 8
            mat, kblk = r // 4, r % 4
            qw_s, sc_s, zkey, W_dst = ((qwg, scg, "g", Wg_dq), (qwu, scu, "u", Wu_dq))[mat]
            n0 = SLW * sl
            if half == 0:
                lb = dqst["lb"] = r % 2
                if dqst["first"]:
                    wait("gpsimd", Z["d_zw"], zw_all)
                    wait("gpsimd", Z["a_xn"], Z["a_xn"].v)
                    dqst["first"] = False
                else:
                    wait("gpsimd", Z["v_dq"], Z["v_dq"].v)
                dqst["qv"] = dma("gpsimd", dq_qw[:, :],
                                 qw_s[128 * kblk:128 * (kblk + 1), n0:n0 + SLW], d_qw[lb])
                dma("gpsimd", dq_z[:, :],
                    z_n[zkey][8 * kblk:8 * kblk + 8, n0:n0 + SLW]
                    .unsqueeze(1).broadcast_to([8, 16, SLW]), d_rep[lb])
                dqst["rv"] = dma("gpsimd", dq_s[:, :],
                                 sc_s[8 * kblk:8 * kblk + 8, n0:n0 + SLW]
                                 .unsqueeze(1).broadcast_to([8, 16, SLW]), d_rep[lb])
                wait("vector", d_qw[lb], dqst["qv"])
                wait("vector", d_rep[lb], dqst["rv"])
            for kbit in range(4 * half, 4 * half + 4):
                tp = kbit % 2
                uv = inc("vector", lambda e, kbit=kbit: e.tensor_scalar(
                    out=dq_w4[:, :], in0=dq_qw[:, :], scalar1=4 * kbit, scalar2=15,
                    op0=ALU.logical_shift_right, op1=ALU.bitwise_and), Z["v_dq"])
                wait("vector", Z["v_dq"], uv)
                wait("vector", d_dqo[tp], d_dqo[tp].v)
                sv = inc("vector", lambda e, tp=tp: e.tensor_tensor(
                    out=dq_T[tp][:, :], in0=dq_w4[:, :], in1=dq_z[:, :],
                    op=ALU.subtract), Z["v_dq"])
                wait("vector", Z["v_dq"], sv)
                mv = inc("vector", lambda e, tp=tp: e.tensor_tensor(
                    out=dq_T[tp][:, :], in0=dq_T[tp][:, :], in1=dq_s[:, :],
                    op=ALU.mult), Z["v_dq"])
                wait("gpsimd", Z["v_dq"], mv)
                c = kbit * 4 + kblk
                dma("gpsimd", W_dst[128 * c:128 * (c + 1), n0:n0 + SLW],
                    dq_T[tp][:, :], d_dqo[tp])
            if half == 1 and idx % 8 == 7:
                dqst["sl_done"][sl] = (Z["d_dqo0"].v, Z["d_dqo1"].v)

        def emit_dequant_down_chunk(m):
            d_qw = [Z["d_qw0"], Z["d_qw1"]]
            d_rep = [Z["d_rep0"], Z["d_rep1"]]
            d_dqo = [Z["d_dqo0"], Z["d_dqo1"]]
            lb = m % 2
            tp = m % 2
            wait("gpsimd", Z["a_xn"], Z["a_xn"].v)
            wait("gpsimd", Z["v_dq"], Z["v_dq"].v)
            qv = dma("gpsimd", dd_qw[:, :],
                     qwd[16 * m:16 * (m + 1), :].unsqueeze(1).broadcast_to([16, 8, D]), d_qw[lb])
            dma("gpsimd", dd_z[:, :], z_n["d"][m:m + 1, :].partition_broadcast(128), d_rep[lb])
            rv = dma("gpsimd", dd_s[:, :], scd[m:m + 1, :].partition_broadcast(128), d_rep[lb])
            wait("vector", d_qw[lb], qv)
            wait("vector", d_rep[lb], rv)
            uv = inc("vector", lambda e: e.tensor_scalar(
                out=dd_w4[:, :], in0=dd_qw[:, :], scalar1=sh8[:, 0:1], scalar2=15,
                op0=ALU.logical_shift_right, op1=ALU.bitwise_and), Z["v_dq"])
            wait("vector", Z["v_dq"], uv)
            wait("vector", d_dqo[tp], d_dqo[tp].v)
            sv = inc("vector", lambda e, tp=tp: e.tensor_tensor(
                out=dd_T[tp][:, :], in0=dd_w4[:, :], in1=dd_z[:, :], op=ALU.subtract), Z["v_dq"])
            wait("vector", Z["v_dq"], sv)
            mv = inc("vector", lambda e, tp=tp: e.tensor_tensor(
                out=dd_T[tp][:, :], in0=dd_T[tp][:, :], in1=dd_s[:, :], op=ALU.mult), Z["v_dq"])
            wait("gpsimd", Z["v_dq"], mv)
            dma("gpsimd", Wd_dq[128 * m:128 * (m + 1), :], dd_T[tp][:, :], d_dqo[tp])

        def emit_p1_matrix(h, w, next_pairs=None):
            d_stat = [Z["d_stat0"], Z["d_stat1"], Z["d_stat2"]]
            d_qkv = [Z["d_qkv0"], Z["d_qkv1"]]
            dst_of = {"q": QT, "k": KT, "v": VT}
            grp0 = Z["t_mm1"].v

            def emit_gather(m):
                grp = grp0 + m
                pb = grp % 3
                if grp >= 3:
                    wait("scalar", Z["t_mm1"], grp - 2)
                return dma("scalar", wstat[pb][:, :].rearrange("p (c j) -> p c j", c=32),
                           wT[w].rearrange("(c p) hd -> p c hd", p=128)[:, :, 128 * m:128 * (m + 1)],
                           d_stat[pb])

            wait("scalar", d_sc[0], d_sc[0].v)
            wait("scalar", d_sc[1], d_sc[1].v)
            sv_of = {0: emit_gather(0), 1: emit_gather(1)}
            for m in range(32):
                grp = Z["t_mm1"].v
                pb = grp % 3
                sv = sv_of[m]
                if next_pairs is not None:
                    emit_wtr(next_pairs, m)
                    emit_wsc(next_pairs, m)
                wait("tensor", d_stat[pb], sv)
                par = grp % 4
                if grp >= 4:
                    wait("tensor", Z["a_ev1"], Z["a_ev1"].v - 3)
                P.em("tensor", lambda e, pb=pb, par=par: [
                    e.matmul(psQ4[par][:, 512 * b:512 * (b + 1)],
                             wstat[pb][:, 128 * c:128 * (c + 1)],
                             xnT(c)[:, 512 * b:512 * (b + 1)],
                             start=(c == 0), stop=(c == NKC - 1))
                    for c in range(NKC) for b in range(2)][-1].then_inc(Z["t_mm1"].h, 1))
                Z["t_mm1"].v += 1
                wait("scalar", Z["t_mm1"], Z["t_mm1"].v)
                sp = grp % 2
                wait("scalar", d_qkv[sp], d_qkv[sp].v)
                ev = inc("scalar", lambda e, par=par, sp=sp, w=w, m=m: e.activation(
                    out=qstg[sp][:, :], in_=psQ4[par][:, :], func=AF.Identity,
                    bias=bt[w][:, m:m + 1], scale=(QSCALE if w == "q" else 1.0)), Z["a_ev1"])
                wait("scalar", Z["a_ev1"], ev)
                dma("scalar", dst_of[w][128 * m:128 * (m + 1), TH * h:TH * (h + 1)],
                    qstg[sp][:, :], d_qkv[sp])
                if m + 2 < 32:
                    sv_of[m + 2] = emit_gather(m + 2)

        def emit_p1_half(h):
            wait("tensor", Z["v_ev"], Z["v_ev"].v)
            if h == 1:
                wait("tensor", Z["a_silu"], Z["a_silu"].v)
                wait("tensor", Z["v_hm"], Z["v_hm"].v)
                wait("scalar", Z["t_mm2"], Z["t_mm2"].v)
                for w in "qkv":
                    emit_p1_matrix(h, w)
            else:
                wait("scalar", Z["v_misc"], vmisc_all)
                wait("scalar", Z["d_zw"], zw_all)
                for a in range(32):
                    emit_wpair("q", a)
                emit_p1_matrix(h, "q", next_pairs="k")
                emit_p1_matrix(h, "k", next_pairs="v")
                emit_p1_matrix(h, "v")

        dm_next = {"v": 0}

        def emit_p2_half(h):
            d_sg = [Z["d_sg0"], Z["d_sg1"]]
            d_su = [Z["d_su0"], Z["d_su1"]]
            sl_waited = [False] * 4
            wait("tensor", Z["a_ev1"], Z["a_ev1"].v)
            wait("sync", Z["t_mm1"], Z["t_mm1"].v)
            for m in range(NM):
                if h == 0 and m < 48:
                    emit_dq_gu_part(8 + m // 2, m % 2)
                mg = Z["v_hm"].v
                par = mg % 2
                sl = min(3, (128 * (m + 1) - 1) // 2752)
                if not sl_waited[sl]:
                    sd = dqst["sl_done"][sl]
                    wait("sync", Z["d_dqo0"], sd[0])
                    wait("sync", Z["d_dqo1"], sd[1])
                    sl_waited[sl] = True
                if mg >= 2:
                    wait("sync", Z["t_mm2"], Z["t_mm2"].v - 2)
                gv = dma("sync", gstat[par][:, :].rearrange("p (c j) -> p c j", c=32),
                         Wg_dq.rearrange("(c p) n -> p c n", p=128)[:, :, 128 * m:128 * (m + 1)],
                         d_sg[par])
                uv = dma("sync", ustat[par][:, :].rearrange("p (c j) -> p c j", c=32),
                         Wu_dq.rearrange("(c p) n -> p c n", p=128)[:, :, 128 * m:128 * (m + 1)],
                         d_su[par])
                wait("tensor", d_sg[par], gv)
                if mg >= 2:
                    wait("tensor", Z["a_silu"], Z["a_silu"].v - 1)
                P.em("tensor", lambda e, par=par: [
                    e.matmul(psG[par][:, 512 * b:512 * (b + 1)],
                             gstat[par][:, 128 * c:128 * (c + 1)],
                             xnT(c)[:, 512 * b:512 * (b + 1)],
                             start=(c == 0), stop=(c == NKC - 1))
                    for c in range(NKC) for b in range(2)][-1].then_inc(Z["t_mm2"].h, 1))
                Z["t_mm2"].v += 1
                gdone = Z["t_mm2"].v
                wait("tensor", d_su[par], uv)
                if mg >= 2:
                    wait("tensor", Z["v_hm"], Z["v_hm"].v - 1)
                P.em("tensor", lambda e, par=par: [
                    e.matmul(psU[par][:, 512 * b:512 * (b + 1)],
                             ustat[par][:, 128 * c:128 * (c + 1)],
                             xnT(c)[:, 512 * b:512 * (b + 1)],
                             start=(c == 0), stop=(c == NKC - 1))
                    for c in range(NKC) for b in range(2)][-1].then_inc(Z["t_mm2"].h, 1))
                Z["t_mm2"].v += 1
                udone = Z["t_mm2"].v
                wait("scalar", Z["t_mm2"], gdone)
                if mg >= 2:
                    wait("scalar", Z["v_hm"], Z["v_hm"].v - 1)
                inc("scalar", lambda e, par=par: e.activation(
                    out=siluG[par][:, :], in_=psG[par][:, :], func=AF.Silu), Z["a_silu"])
                wait("vector", Z["t_mm2"], udone)
                wait("vector", Z["a_silu"], Z["a_silu"].v)
                sp = mg % 2
                wait("vector", Z["d_ht_w"], Z["d_ht_w"].v)
                hv = inc("vector", lambda e, par=par, sp=sp: e.tensor_tensor(
                    out=hstg[sp][:, :], in0=psU[par][:, :], in1=siluG[par][:, :], op=ALU.mult),
                    Z["v_hm"])
                wait("scalar", Z["v_hm"], hv)
                dma("scalar", HT[128 * m:128 * (m + 1), TH * h:TH * (h + 1)], hstg[sp][:, :],
                    Z["d_ht_w"])
                if ((h == 0 and m >= 49 and m % 2 == 1) or h == 1) and dm_next["v"] < NM:
                    emit_dequant_down_chunk(dm_next["v"])
                    dm_next["v"] += 1

        psD4 = [ps[:, 1024 * c:1024 * (c + 1)] for c in range(4)]
        scw = ps[:, 2048:3072]
        vpsw = ps[:, 3584:4096].bitcast(BF16)

        def QTb(s):
            return R1[:, 16384 * s + 0:16384 * s + 4096]

        def KTb(s):
            return R1[:, 16384 * s + 4096:16384 * s + 8192]

        def VTb(s):
            return R1[:, 16384 * s + 8192:16384 * s + 12288]

        def sTb(s):
            return R1[:, 16384 * s + 12288:16384 * s + 16384]

        def emit_bank_load(sb, first=False):
            s = sb % 2
            if not first:
                wait("sync", Z["g_stk"], Z["g_stk"].v)
                wait("sync", Z["a_vst"], Z["a_vst"].v)
            lv = 0
            for (dstf, srcd) in ((QTb, QT), (KTb, KT), (VTb, VT)):
                lv = dma("sync", dstf(s).rearrange("p (c t) -> p c t", c=32),
                         srcd.rearrange("(c p) t -> p c t", p=128)[:, :, 128 * sb:128 * (sb + 1)],
                         Z["d_bank"])
            return lv

        def emit_p4():
            d_hr = [Z["d_hr0"], Z["d_hr1"], Z["d_hr2"]]
            d_wd = [Z["d_wd0"], Z["d_wd1"]]
            d_gt = [Z["d_gt0"], Z["d_gt1"]]
            wait("gpsimd", Z["d_ht_w"], Z["d_ht_w"].v)
            wait("sync", Z["d_dqo0"], Z["d_dqo0"].v)
            wait("sync", Z["d_dqo1"], Z["d_dqo1"].v)
            wait("sync", Z["t_mm2"], Z["t_mm2"].v)
            wait("tensor", Z["a_silu"], Z["a_silu"].v)
            wait("tensor", Z["v_hm"], Z["v_hm"].v)
            wait("tensor", Z["a_ev1"], Z["a_ev1"].v)
            for th in range(2):
                for hq in range(8):
                    g2 = th * 8 + hq
                    hp0 = 512 * hq
                    bat_v = {}

                    def load_bat(gg, hp0=hp0, g2=g2):
                        pb = (g2 * 11 + gg) % 2
                        wait("sync", Z["t_mm4"], Z["t_mm4"].v)
                        ng = min(8, NM - 8 * gg)
                        bat_v[gg] = dma(
                            "sync", wdbat[pb][:, 0:512 * ng].rearrange("p (g n) -> p g n", g=ng),
                            Wd_dq.rearrange("(g p) dd -> p g dd", p=128)[:, 8 * gg:8 * gg + ng,
                                                                        hp0:hp0 + 512], d_wd[pb])
                    load_bat(0)
                    hv_of = {}
                    for m in range(NM):
                        if m % 8 == 0 and m // 8 + 1 <= 10:
                            load_bat(m // 8 + 1)
                        if m % 2 == 0:
                            hb2 = (43 * g2 + m // 2) % 3
                            wait("gpsimd", Z["t_mm4"], max(0, Z["t_mm4"].v - 4))
                            hv_of[m // 2] = dma(
                                "gpsimd", htb[hb2][:, 0:2048].rearrange("p (c t) -> p c t", c=2),
                                HT[128 * m:128 * (m + 2),
                                   1024 * th:1024 * (th + 1)].rearrange(
                                    "(c p) t -> p c t", p=128), d_hr[hb2])
                        hb = (43 * g2 + m // 2) % 3
                        hoff = 1024 * (m % 2)
                        gg, gi = m // 8, m % 8
                        pb = (g2 * 11 + gg) % 2
                        wait("tensor", d_wd[pb], bat_v[gg])
                        wait("tensor", d_hr[hb], hv_of[m // 2])
                        if m == 0:
                            for c4 in range(4):
                                if g2 > 0:
                                    wait("tensor", Z["a_silu4"], 4 * (g2 - 1) + c4 + 1)
                                last = (c4 == 3)
                                P.em("tensor", lambda e, gi=gi, pb=pb, hb=hb, hoff=hoff,
                                     c4=c4, last=last: [
                                    (ins.then_inc(Z["t_mm4"].h, 1) if (last and b == 1) else ins)
                                    for b in range(2)
                                    for ins in [e.matmul(
                                        psD4[c4][:, 512 * b:512 * (b + 1)],
                                        wdbat[pb][:, 512 * gi + 128 * c4:512 * gi + 128 * (c4 + 1)],
                                        htb[hb][:, hoff + 512 * b:hoff + 512 * (b + 1)],
                                        start=True, stop=False)]][-1])
                            Z["t_mm4"].v += 1
                        else:
                            P.em("tensor", lambda e, m=m, gi=gi, pb=pb, hb=hb, hoff=hoff: [
                                e.matmul(psD4[c4][:, 512 * b:512 * (b + 1)],
                                         wdbat[pb][:, 512 * gi + 128 * c4:512 * gi + 128 * (c4 + 1)],
                                         htb[hb][:, hoff + 512 * b:hoff + 512 * (b + 1)],
                                         start=False, stop=(m == NM - 1))
                                for c4 in range(4) for b in range(2)][-1].then_inc(Z["t_mm4"].h, 1))
                            Z["t_mm4"].v += 1
                    wait("scalar", Z["t_mm4"], Z["t_mm4"].v)
                    for c4 in range(4):
                        sp = c4 % 2
                        wait("scalar", d_gt[sp], d_gt[sp].v)
                        sv = inc("scalar", lambda e, c4=c4, sp=sp: e.activation(
                            out=gtstg[sp][:, :], in_=psD4[c4][:, :], func=AF.Silu),
                            Z["a_silu4"])
                        wait("scalar", Z["a_silu4"], sv)
                        dma("scalar", GT[128 * (4 * hq + c4):128 * (4 * hq + c4 + 1),
                                         1024 * th:1024 * (th + 1)], gtstg[sp][:, :], d_gt[sp])

        ctxw_f = ps[:, 3072:3584]

        trp5 = [ps[:, 0:64].bitcast(BF16), ps[:, 512:576].bitcast(BF16)]

        def gTl(pb):
            return SB[:, 4096 * pb:4096 * (pb + 1)]

        def sg5(pb):
            return SB[:, 8192 + 4096 * pb:8192 + 4096 * (pb + 1)]

        qkwB = [qkw, qstg[0]]
        kkwB = [kkw, qstg[1]]
        vpwB = [vpw, siluG[0]]
        vkwB = [vkw, siluG[1]]
        PTwB = [PTw, hstg[0]]
        scwB = [ps[:, 2048:3072], ps[:, 1024:2048]]

        def emit_ab_scores(s, bb, nb, bank_lv):
            w = nb % 2
            wait("gpsimd", Z["d_bank"], bank_lv)
            wait("gpsimd", Z["t_sc"], max(0, nb - 1))
            for (dst, srcf) in ((qkwB[w], QTb), (kkwB[w], KTb)):
                inc("gpsimd", lambda e, dst=dst, srcf=srcf, s=s, bb=bb: e.tensor_copy(
                    out=dst[:, :].rearrange("p (a b) -> p a b", a=32, b=32),
                    in_=srcf(s).rearrange("p (b t) -> p t b", b=32)[:, 32 * bb:32 * (bb + 1), :]),
                    Z["g_stk"])
            wait("scalar", Z["d_bank"], bank_lv)
            wait("scalar", Z["t_vt"], max(0, nb - 1))
            inc("scalar", lambda e, s=s, bb=bb, w=w: e.copy(
                out=vpwB[w][:, :].rearrange("p (a b) -> p a b", a=32, b=32),
                in_=VTb(s).rearrange("p (b t) -> p t b", b=32)[:, 32 * bb:32 * (bb + 1), :]),
                Z["a_vst"])
            wait("tensor", Z["g_stk"], Z["g_stk"].v)
            wait("tensor", Z["a_exp"], max(0, nb - 1))
            P.em("tensor", lambda e, w=w: [
                (e.matmul(scwB[w][:, 128 * gg:128 * (gg + 1)],
                          qkwB[w][:, 128 * gg:128 * (gg + 1)],
                          kkwB[w][:, 128 * gg:128 * (gg + 1)], start=True, stop=False),
                 e.matmul(scwB[w][:, 128 * gg:128 * (gg + 1)], mskl[:, :], mskr[:, :],
                          start=False, stop=True))
                for gg in range(8)][-1][-1].then_inc(Z["t_sc"].h, 1))
            Z["t_sc"].v += 1
            wait("scalar", Z["t_sc"], Z["t_sc"].v)
            wait("scalar", Z["v_p3"], max(0, 4 * nb - 1))
            ev = inc("scalar", lambda e, w=w: e.activation(
                out=Ew[:, :], in_=scwB[w][:, :], func=AF.Exp, bias=0.0, scale=1.0), Z["a_exp"])
            wait("vector", Z["a_exp"], ev)
            rv = inc("vector", lambda e: e.reduce_sum(
                out=sumw[:, :].rearrange("p (g o) -> p g o", o=1),
                in_=Ew[:, :].rearrange("p (g t) -> p g t", g=8), axis=AX), Z["v_p3"])
            wait("vector", Z["v_p3"], rv)
            cv = inc("vector", lambda e: e.reciprocal(out=rcpw[:, :], in_=sumw[:, :]), Z["v_p3"])
            wait("vector", Z["v_p3"], cv)
            mv = inc("vector", lambda e: e.tensor_tensor(
                out=Pw[:, :].rearrange("p (g t) -> p g t", g=8),
                in0=Ew[:, :].rearrange("p (g t) -> p g t", g=8),
                in1=rcpw[:, :].rearrange("p (g o) -> p g o", o=1).broadcast_to([128, 8, 128]),
                op=ALU.mult), Z["v_p3"])
            wait("vector", Z["v_p3"], mv)
            wait("vector", Z["t_cx"], max(0, 2 * (nb - 1)))
            inc("vector", lambda e, w=w: e.transpose(out=PTwB[w][:, :], in_=Pw[:, :]), Z["v_p3"])

        def emit_ab_vtr(s, bb, nb):
            w = nb % 2
            wait("tensor", Z["a_vk"], nb)
            wait("tensor", Z["a_vst"], Z["a_vst"].v)
            P.em("tensor", lambda e, w=w: [
                e.transpose(vpsw[:, 128 * gg:128 * (gg + 1)],
                            vpwB[w][:, 128 * gg:128 * (gg + 1)], id_b[:, :])
                for gg in range(8)][-1].then_inc(Z["t_vt"].h, 1))
            Z["t_vt"].v += 1
            wait("scalar", Z["t_vt"], Z["t_vt"].v)
            wait("scalar", Z["t_cx"], max(0, 2 * (nb - 1)))
            inc("scalar", lambda e, w=w: e.copy(out=vkwB[w][:, :], in_=vpsw[:, :]), Z["a_vk"])

        def emit_ab_ctx(s, bb, nb):
            w = nb % 2
            wait("tensor", Z["a_vk"], nb + 1)
            wait("tensor", Z["v_p3"], 4 * (nb + 1))
            for h2 in range(2):
                wait("tensor", Z["v_dep"], 2 * nb + h2)
                P.em("tensor", lambda e, h2=h2, w=w: [
                    e.matmul(ctxw_f[:, 128 * g4:128 * (g4 + 1)],
                             vkwB[w][:, 128 * (4 * h2 + g4):128 * (4 * h2 + g4 + 1)],
                             PTwB[w][:, 128 * (4 * h2 + g4):128 * (4 * h2 + g4 + 1)],
                             start=True, stop=True)
                    for g4 in range(4)][-1].then_inc(Z["t_cx"].h, 1))
                Z["t_cx"].v += 1
                wait("vector", Z["t_cx"], Z["t_cx"].v)
                inc("vector", lambda e, s=s, bb=bb, h2=h2: e.tensor_copy(
                    out=sTb(s).rearrange("p (b t) -> p t b", b=32)[:, 32 * bb + 16 * h2:
                                                                   32 * bb + 16 * h2 + 16, :],
                    in_=ctxw_f[:, :].rearrange("p (t b) -> p t b", t=16, b=32)), Z["v_dep"])

        def emit_p5_tile(ti):
            pb = ti % 2
            if ti >= 2:
                wait("sync", Z["t_t5"], Z["t_t5"].v)
                wait("sync", Z["v_m5"], max(0, Z["v_m5"].v - 1))
            lv = dma("sync", gTl(pb)[:, :].rearrange("p (c t) -> p c t", c=32),
                     GT.rearrange("(c p) t -> p c t", p=128)[:, :, 128 * ti:128 * (ti + 1)],
                     Z["d_bank"])
            wait("sync", Z["v_a5"], max(0, Z["v_a5"].v - 1))
            xlv = dma("sync", xbuf[pb][:, :], x_in[128 * ti:128 * (ti + 1), :], d_x[pb])
            wait("vector", Z["d_bank"], lv)
            wait("vector", Z["v_dep"], 8 * (ti + 1))
            mv = inc("vector", lambda e, pb=pb, ti=ti: e.tensor_tensor(
                out=sg5(pb)[:, :], in0=sTb(ti % 2)[:, :], in1=gTl(pb)[:, :], op=ALU.mult),
                Z["v_m5"])
            wait("tensor", Z["v_m5"], mv)
            if ti >= 2:
                wait("scalar", Z["d_out"], Z["d_out"].v)
            if ti == 0:
                wait("scalar", Z["d_dqo0"], Z["d_dqo0"].v)
                wait("scalar", Z["d_dqo1"], Z["d_dqo1"].v)
            for hh in range(H):
                tb = hh % 2
                wait("tensor", Z["a_e5"], max(0, Z["a_e5"].v - 1))
                inc("tensor", lambda e, hh=hh, pb=pb, tb=tb: e.transpose(
                    trp5[tb][:, :], sg5(pb)[:, 128 * hh:128 * (hh + 1)], id_b[:, :]),
                    Z["t_t5"])
                wait("scalar", Z["t_t5"], Z["t_t5"].v)
                inc("scalar", lambda e, hh=hh, pb=pb, tb=tb: e.copy(
                    out=outb[pb][:, 128 * hh:128 * (hh + 1)], in_=trp5[tb][:, :]), Z["a_e5"])
            wait("gpsimd", Z["a_e5"], Z["a_e5"].v)
            wait("gpsimd", d_x[pb], xlv)
            av = inc("gpsimd", lambda e, pb=pb: e.tensor_tensor(
                out=outb[pb][:, :], in0=xbuf[pb][:, :], in1=outb[pb][:, :],
                op=ALU.add), Z["v_a5"])
            wait("scalar", Z["v_a5"], av)
            dma("scalar", out_t[128 * ti:128 * (ti + 1), :], outb[pb][:, :], Z["d_out"])

        def emit_p3p5():
            wait("sync", Z["t_mm4"], Z["t_mm4"].v)
            wait("sync", Z["d_qkv0"], Z["d_qkv0"].v)
            wait("sync", Z["d_qkv1"], Z["d_qkv1"].v)
            wait("sync", Z["d_gt0"], Z["d_gt0"].v)
            wait("sync", Z["d_gt1"], Z["d_gt1"].v)
            wait("tensor", Z["a_silu4"], Z["a_silu4"].v)
            blv = {0: emit_bank_load(0, first=True)}
            blv[1] = emit_bank_load(1, first=True)
            nb = 0
            for sb in range(16):
                s = sb % 2
                if sb >= 1 and sb + 1 <= 15:
                    blv[sb + 1] = emit_bank_load(sb + 1)
                for bb in range(4):
                    if nb >= 1:
                        pbb = (nb - 1) % 4
                        ps_ = ((nb - 1) // 4) % 2
                        emit_ab_ctx(ps_, pbb, nb - 1)
                    emit_ab_scores(s, bb, nb, blv[sb])
                    emit_ab_vtr(s, bb, nb)
                    nb += 1
                if sb >= 1 and not skip_p5:
                    emit_p5_tile(sb - 1)
            emit_ab_ctx((15) % 2, 3, nb - 1)
            if not skip_p5:
                emit_p5_tile(15)
                wait("scalar", Z["d_out"], Z["d_out"].v)
            else:
                wait("scalar", Z["v_dep"], Z["v_dep"].v)
                for sq in range(2):
                    dma("scalar", sT_h[:, 4096 * sq:4096 * (sq + 1)], sTb(sq)[:, :], Z["d_stw"])
                wait("scalar", Z["d_stw"], Z["d_stw"].v)
                for ti in range(16):
                    dma("scalar", out_t[128 * ti:128 * (ti + 1), :], R2[:, 0:4096], Z["d_out"])
                wait("scalar", Z["d_out"], Z["d_out"].v)

        emit_p0c_half(0)
        for idx in range(8):
            emit_dq_gu_part(idx, 0)
            emit_dq_gu_part(idx, 1)
        emit_p1_half(0)
        emit_p2_half(0)
        emit_p0c_half(1)
        emit_p1_half(1)
        emit_p2_half(1)
        emit_p4()
        if skip_tail:
            wait("scalar", Z["d_gt0"], Z["d_gt0"].v)
            wait("scalar", Z["d_gt1"], Z["d_gt1"].v)
            for ti in range(16):
                dma("scalar", out_t[128 * ti:128 * (ti + 1), :], R2[:, 0:4096], Z["d_out"])
            wait("scalar", Z["d_out"], Z["d_out"].v)
        else:
            emit_p3p5()

        block = S.enter_context(nc.Block())

        @block.sync
        def _(e):
            for f in P.ops["sync"]:
                f(e)

        @block.scalar
        def _(e):
            for f in P.ops["scalar"]:
                f(e)

        @block.vector
        def _(e):
            for f in P.ops["vector"]:
                f(e)

        @block.tensor
        def _(e):
            for f in P.ops["tensor"]:
                f(e)

        @block.gpsimd
        def _(e):
            for f in P.ops["gpsimd"]:
                f(e)

    return nc


def _consts():
    identb = np.eye(128, dtype=ml_dtypes.bfloat16)
    identf = np.eye(128, dtype=np.float32)
    BIG = np.float32(32768.0)
    ml_ = np.zeros((5, 128), dtype=np.float32)
    mr_ = np.zeros((5, 128), dtype=np.float32)
    ml_[0, :] = 1.0
    mr_[0, :] = -BIG
    for mm in range(4):
        ml_[1 + mm, 32 * mm:32 * (mm + 1)] = 1.0
        mr_[1 + mm, 32 * mm:32 * (mm + 1)] = BIG
    maskl = ml_.astype(ml_dtypes.bfloat16)
    maskr = mr_.astype(ml_dtypes.bfloat16)
    shift8 = (4 * (np.arange(128, dtype=np.int32) % 8)).reshape(128, 1)
    return identb, identf, maskl, maskr, shift8


def make_inputs(inputs):
    identb, identf, maskl, maskr, shift8 = _consts()
    ca = np.ascontiguousarray
    common = {
        "wq": ca(np.asarray(inputs["L0_attn_wq"], np.float32)),
        "wk": ca(np.asarray(inputs["L0_attn_wk"], np.float32)),
        "wv": ca(np.asarray(inputs["L0_attn_wv"], np.float32)),
        "bq": ca(np.asarray(inputs["L0_attn_bq"], np.float32)),
        "bk": ca(np.asarray(inputs["L0_attn_bk"], np.float32)),
        "bv": ca(np.asarray(inputs["L0_attn_bv"], np.float32)),
        "qwg": ca(np.asarray(inputs["L0_ffn_gate_qweight"], np.int32)),
        "scg": ca(np.asarray(inputs["L0_ffn_gate_scales"], np.float32)),
        "qzg": ca(np.asarray(inputs["L0_ffn_gate_qzeros"], np.int32)),
        "qwu": ca(np.asarray(inputs["L0_ffn_up_qweight"], np.int32)),
        "scu": ca(np.asarray(inputs["L0_ffn_up_scales"], np.float32)),
        "qzu": ca(np.asarray(inputs["L0_ffn_up_qzeros"], np.int32)),
        "qwd": ca(np.asarray(inputs["L0_ffn_down_qweight"], np.int32)),
        "scd": ca(np.asarray(inputs["L0_ffn_down_scales"], np.float32)),
        "qzd": ca(np.asarray(inputs["L0_ffn_down_qzeros"], np.int32)),
        "identb": identb, "identf": identf, "maskl": maskl, "maskr": maskr, "shift8": shift8,
    }
    x = np.asarray(inputs["L0_attn_hidden_states"], dtype=np.float32).reshape(16384, D)
    in_maps = []
    for c in range(NCORES):
        m = dict(common)
        m["x"] = np.ascontiguousarray(x[T * c:T * (c + 1)])
        in_maps.append(m)
    return in_maps


def kernel(**inputs):
    in_maps = make_inputs(inputs)
    nc = build_nc(debug=False)
    res = run_bass_kernel_spmd(nc, in_maps, core_ids=list(range(NCORES)))
    out = np.concatenate([np.asarray(r["out"], np.float32) for r in res.results], axis=0)
    return out.reshape(2, 8192, D)
